# version 24
# baseline (speedup 1.0000x reference)
import contextlib
import numpy as np
import ml_dtypes
import concourse.bass as bass
import concourse.mybir as mybir
from concourse.bass_utils import run_bass_kernel_spmd

F32 = mybir.dt.float32
BF16 = mybir.dt.bfloat16
U8 = mybir.dt.uint8
AF = mybir.ActivationFunctionType
ALU = mybir.AluOpType
BF = ml_dtypes.bfloat16

DBG_PHASE = None

DM = 2048
SEQ = 4096
TC = 512
NCH = SEQ // TC
NDC = DM // 128
NFC = 4096 // 128
NST = 16
NQ = 4
GN = NST // NQ
WCOL = 2176
EPS = 1e-5


class Dep:
    __slots__ = ("d",)

    def __init__(self):
        self.d = {}

    def add(self, sem, val):
        if val and self.d.get(sem, 0) < val:
            self.d[sem] = val


class Buf:
    __slots__ = ("rd", "wr")

    def __init__(self):
        self.rd = Dep()
        self.wr = Dep()


class Prog:
    def __init__(self):
        self.ops = {e: [] for e in "TSVGY"}
        self.ct = {"T": 0, "S": 0, "V": 0, "G": 0, "D": 0, "O": 0, "C": 0}
        self.waited = {e: {} for e in "TSVGY"}

    def wait(self, eng, sem, val):
        if val <= 0:
            return
        w = self.waited[eng]
        if w.get(sem, 0) >= val:
            return
        w[sem] = val
        self.ops[eng].append(("w", sem, val))

    def raw_emit(self, eng, fn, inc, amt):
        pos = None
        if inc is not None:
            self.ct[inc] += amt
            pos = self.ct[inc]
        self.ops[eng].append(("i", fn, inc, amt))
        return pos


ESEM = {"T": "T", "S": "S", "V": "V", "G": "G"}


class Gen:

    def __init__(self):
        self.p = Prog()
        self.pending_units = []
        self.pending_prio = []

    def op(self, eng, fns, reads=(), writes=(), inc=True, amt=1, sem=None):
        p = self.p
        dep = Dep()
        for b in reads:
            for s, v in b.wr.d.items():
                dep.add(s, v)
        for b in writes:
            for s, v in b.rd.d.items():
                dep.add(s, v)
            for s, v in b.wr.d.items():
                dep.add(s, v)
        for s, v in sorted(dep.d.items()):
            p.wait(eng, s, v)
        if not isinstance(fns, (list, tuple)):
            fns = [fns]
        pos = None
        if eng == "Y":
            for fn in fns[:-1]:
                p.raw_emit(eng, fn, None, 0)
            pos = p.raw_emit(eng, fns[-1], sem or "D", 16)
            s_used = sem or "D"
        else:
            s_used = ESEM[eng]
            for fn in fns[:-1]:
                p.raw_emit(eng, fn, None, 0)
            if inc:
                pos = p.raw_emit(eng, fns[-1], s_used, amt)
            else:
                p.raw_emit(eng, fns[-1], None, 0)
        for b in reads:
            if pos:
                b.rd.add(s_used, pos)
        for b in writes:
            if pos:
                b.wr = Dep()
                b.wr.add(s_used, pos)
                b.rd = Dep()
        return pos

    def push_unit(self, fn):
        self.pending_units.append(fn)

    def push_prio(self, fn):
        self.pending_prio.append(fn)

    def pop_tasks(self, k):
        n = 0
        while self.pending_prio and n < k:
            self.pending_prio.pop(0)()
            n += 1
        while self.pending_units and n < k:
            self.pending_units.pop(0)()
            n += 1

    def flush(self):
        while self.pending_prio:
            self.pending_prio.pop(0)()
        while self.pending_units:
            self.pending_units.pop(0)()


def build_program():
    nc = bass.Bass("TRN2", target_bir_lowering=False)

    xT = nc.dram_tensor("xT", [DM, SEQ], F32, kind="ExternalInput").ap()
    yT = nc.dram_tensor("yT", [DM, SEQ], F32, kind="ExternalOutput").ap()
    wah = [nc.dram_tensor(f"wa{e}", [32, 128, WCOL], BF16, kind="ExternalInput").ap()
           for e in range(2)]
    wbh = [nc.dram_tensor(f"wb{e}", [16, 2, 128, WCOL], BF16, kind="ExternalInput").ap()
           for e in range(2)]
    xpw_h = nc.dram_tensor("xpw_h", [128, NDC * 160], BF16, kind="ExternalInput").ap()
    dtw_h = nc.dram_tensor("dtw_h", [128, DM], BF16, kind="ExternalInput").ap()
    wgf_h = nc.dram_tensor("wgf_h", [128, NDC * 2], F32, kind="ExternalInput").ap()
    selc_h = nc.dram_tensor("selc_h", [32, 32], BF16, kind="ExternalInput").ap()
    alog_h = nc.dram_tensor("alog_h", [128, NDC * NST], F32, kind="ExternalInput").ap()
    dvec_h = nc.dram_tensor("dvec_h", [128, NDC], F32, kind="ExternalInput").ap()
    dtb_h = nc.dram_tensor("dtb_h", [128, NDC], F32, kind="ExternalInput").ap()
    b0a_h = nc.dram_tensor("b0a_h", [128, NFC], F32, kind="ExternalInput").ap()
    b1a_h = nc.dram_tensor("b1a_h", [128, NFC], F32, kind="ExternalInput").ap()
    onesb_h = nc.dram_tensor("onesb_h", [1, TC], BF16, kind="ExternalInput").ap()
    on1f_h = nc.dram_tensor("on1f_h", [1, 128], F32, kind="ExternalInput").ap()
    onesk_h = nc.dram_tensor("onesk_h", [128, 1], F32, kind="ExternalInput").ap()

    with contextlib.ExitStack() as st:
        def sb(name, shape, dt=F32):
            return st.enter_context(nc.sbuf_tensor(name, shape, dt))

        xb = [sb("xb0", [128, NDC * TC]), sb("xb1", [128, NDC * TC])]
        xnb = [sb("xnb0", [128, NDC * TC], BF16), sb("xnb1", [128, NDC * TC], BF16)]
        hbuf = sb("hbuf", [128, NFC * TC], BF16)
        out0 = sb("out0", [128, NDC * TC], BF16)
        out1 = sb("out1", [128, TC], BF16)
        wr = [sb(f"wr{i}", [128, WCOL], BF16) for i in range(3)]
        xpw = sb("xpw", [128, NDC * 160], BF16)
        dtw = sb("dtw", [128, DM], BF16)
        wgf = sb("wgf", [128, NDC * 2])
        sdif = sb("sdif", [1, 1])
        selc = sb("selc", [32, 32], BF16)
        negA = sb("negA", [128, NDC * NST])
        dvec = sb("dvec", [128, NDC])
        dtb = sb("dtb", [128, NDC])
        b0a = sb("b0a", [128, NFC])
        b1a = sb("b1a", [128, NFC])
        onesb = sb("onesb", [1, TC], BF16)
        on1b = onesb[0:1, 0:128]
        on1f = sb("on1f", [1, 128])
        onesk = sb("onesk", [128, 1])
        drs = sb("drs", [128, TC], BF16)
        bcs = sb("bcs", [32, TC], BF16)
        Bbh = sb("Bbh", [128, GN * TC], BF16)
        Cbh = sb("Cbh", [128, GN * TC], BF16)
        stg = out1[0:1, :]
        delta_t = sb("delta_t", [128, TC])
        du_t = sb("du_t", [128, TC], BF16)
        RDA = 3
        dA_r = [sb(f"dA{i}", [128, TC]) for i in range(RDA)]
        dBu_r = [sb("dBu0", [128, TC], BF16), sb("dBu1", [128, TC], BF16)]
        h_r = [sb("h0", [128, TC], BF16), sb("h1", [128, TC], BF16)]
        ymt = [sb("ymt0", [128, TC], BF16), sb("ymt1", [128, TC], BF16)]
        h_state = sb("h_state", [128, NDC * NST])
        sp_a = sb("sp_a", [128, TC])
        sp_b = sb("sp_b", [128, TC])
        sq = [sb("sq0", [128, TC], BF16), sb("sq1", [128, TC], BF16)]
        st2 = sb("st2", [1, TC])
        onesk_bf = sb("onesk_bf", [128, 1], BF16)
        mask_i = sb("mask_i", [128, TC], U8)

        ps = [st.enter_context(nc.psum_tensor(f"p{i}", [128, TC], F32))
              for i in range(8)]

        if globals().get("PROBE_SBUF"):
            lo, hi = 0, 300000
            while lo < hi - 64:
                mid = (lo + hi) // 2
                try:
                    cm = nc.sbuf_tensor("probe_x", [128, mid], U8)
                    cm.__enter__()
                    cm.__exit__(None, None, None)
                    lo = mid
                except Exception:
                    hi = mid
            print(f"SBUF headroom: {lo} bytes/partition")

        sems = {}
        for s in ("T", "S", "V", "G", "D", "O", "C"):
            sems[s] = st.enter_context(nc.semaphore(f"sem{s}"))
        block = st.enter_context(nc.Block())

        s0 = sp_a[0:1, :]
        s1 = sp_b[0:1, :]
        s2v = st2[0:1, :]

        g = Gen()

        bX = [[Buf() for _ in range(NDC)], [Buf() for _ in range(NDC)]]
        bXN = [Buf(), Buf()]
        bH = Buf()
        bO0 = Buf()
        bO1 = Buf()
        bWr = [Buf(), Buf(), Buf()]
        bP = [Buf() for _ in range(8)]
        bDrs, bBcs = Buf(), Buf()
        bBbh, bCbh = Buf(), Buf()
        bDelta, bDu = Buf(), Buf()
        bDA = [Buf() for _ in range(RDA)]
        bDBu = [Buf(), Buf()]
        bHr = [Buf(), Buf()]
        bYmt = [Buf(), Buf()]
        bSpA = Buf()
        bSpB = Buf()
        bSq = [Buf(), Buf()]
        bS0, bS1, bS2 = Buf(), Buf(), Buf()
        bMask = Buf()
        bConst = Buf()
        bPosA, bNdtb = Buf(), Buf()
        bHst = Buf()

        savepos = {}
        unit_ctr = [0]
        gctr = [0]

        def dma(dst_ap, src_ap):
            return lambda e: e.dma_start(out=dst_ap, in_=src_ap)

        const_loads = [
            (xpw[:, :], xpw_h), (dtw[:, :], dtw_h), (wgf[:, :], wgf_h),
            (selc[:, :], selc_h), (negA[:, :], alog_h), (dvec[:, :], dvec_h),
            (dtb[:, :], dtb_h), (b0a[:, :], b0a_h), (b1a[:, :], b1a_h),
            (onesb[:, :], onesb_h), (on1f[:, :], on1f_h),
            (onesk[:, :], onesk_h),
        ]
        for d, s in const_loads:
            g.op("Y", dma(d, s), writes=[bConst], sem="C")
        g.op("S", lambda e: e.activation(out=sp_a[:, 0:NDC * NST], in_=negA[:, :],
                                         func=AF.Exp),
             reads=[bConst], writes=[bSpA])
        g.op("S", lambda e: e.mul(negA[:, :], sp_a[:, 0:NDC * NST], -1.0),
             reads=[bSpA], writes=[bPosA])
        g.op("V", lambda e: e.memset(h_state[:, :], 0.0), writes=[bHst])
        g.op("V", lambda e: e.tensor_copy(onesk_bf[:, :], onesk[:, :]),
             reads=[bConst], writes=[bConst])
        nc.const_aps.register(F32, 1.0, onesk[:, :])
        bSdif = Buf()
        for e_i, bank in ((0, 2), (1, 3)):
            def sfn(k, e_i_, bank_):
                return lambda e: e.matmul(
                    ps[bank_][0:1, 0:1], wgf[:, k * 2 + e_i_:k * 2 + e_i_ + 1],
                    onesk[:, :], start=(k == 0), stop=(k == NDC - 1))
            g.op("T", [sfn(k, e_i, bank) for k in range(NDC)],
                 reads=[bConst], writes=[bP[bank]])
        g.op("S", lambda e: e.copy(st2[0:1, 0:1], ps[2][0:1, 0:1]),
             reads=[bP[2]], writes=[bS2])
        g.op("V", lambda e: e.tensor_tensor(sdif[:, :], st2[0:1, 0:1],
                                            ps[3][0:1, 0:1], ALU.subtract),
             reads=[bS2, bP[3]], writes=[bSdif])

        xT3 = xT.rearrange("(dc p) t -> p dc t", p=128)
        yT3 = yT.rearrange("(dc p) t -> p dc t", p=128)

        def load_x(c):
            buf = xb[c % 2]
            b3 = buf.rearrange("p (dc t) -> p dc t", dc=NDC)
            for gq in range(4):
                g.op("Y", dma(b3[:, gq * 4:(gq + 1) * 4, :],
                              xT3[:, gq * 4:(gq + 1) * 4, c * TC:(c + 1) * TC]),
                     writes=bX[c % 2][gq * 4:(gq + 1) * 4])

        def store_x(c):
            while g.pending_prio:
                g.pending_prio.pop(0)()
            buf = xb[c % 2]
            b3 = buf.rearrange("p (dc t) -> p dc t", dc=NDC)
            for gq in range(4):
                g.op("Y", dma(yT3[:, gq * 4:(gq + 1) * 4, c * TC:(c + 1) * TC],
                              b3[:, gq * 4:(gq + 1) * 4, :]),
                     reads=bX[c % 2][gq * 4:(gq + 1) * 4], sem="O")

        def layernorm(c):
            x = xb[c % 2]
            xn = xnb[c % 2]
            bx, bxn = bX[c % 2], bXN[c % 2]
            def s1fn(k):
                return lambda e: e.matmul(ps[2][0:1, :], onesk[:, :],
                                          x[:, k * TC:(k + 1) * TC],
                                          start=(k == 0), stop=(k == NDC - 1))
            for k in range(NDC):
                g.op("T", s1fn(k), reads=[bx[k], bConst], writes=[bP[2]])
            for k in range(NDC):
                sl = sq[k % 2]
                g.op("S", (lambda sl_, k_: lambda e: e.activation(
                    out=sl_[:, :], in_=x[:, k_ * TC:(k_ + 1) * TC], func=AF.Square))(sl, k),
                    reads=[bx[k]], writes=[bSq[k % 2]])
                g.op("T", (lambda sl_, k_: lambda e: e.matmul(
                    ps[3][0:1, :], onesk_bf[:, :], sl_[:, :],
                    start=(k_ == 0), stop=(k_ == NDC - 1)))(sl, k),
                    reads=[bSq[k % 2], bConst], writes=[bP[3]])
            g.op("V", lambda e: e.tensor_scalar(s0, ps[2][0:1, :], 1.0 / DM,
                                                None, ALU.mult),
                 reads=[bP[2]], writes=[bS0, bSpA])
            g.op("V", lambda e: e.tensor_scalar(s1, ps[3][0:1, :], 1.0 / DM,
                                                None, ALU.mult),
                 reads=[bP[3]], writes=[bS1, bSpB])
            g.op("V", lambda e: e.tensor_tensor(s2v, s0, s0, ALU.mult),
                 reads=[bS0], writes=[bS2])
            g.op("V", lambda e: e.tensor_tensor(s1, s1, s2v, ALU.subtract),
                 reads=[bS2], writes=[bS1, bSpB])
            g.op("V", lambda e: e.tensor_scalar(s1, s1, EPS, None, ALU.add),
                 writes=[bS1, bSpB])
            g.op("S", lambda e: e.activation(out=s2v, in_=s1, func=AF.Ln),
                 reads=[bS1], writes=[bS2])
            g.op("S", lambda e: e.activation(out=s1, in_=s2v, func=AF.Exp,
                                             scale=-0.5),
                 reads=[bS2], writes=[bS1, bSpB])
            g.op("V", lambda e: e.tensor_tensor(s2v, s0, s1, ALU.mult),
                 reads=[bS0, bS1], writes=[bS2])
            g.op("T", lambda e: e.matmul(ps[4][:, :], on1f[:, :], s2v,
                                         start=True, stop=True),
                 reads=[bS2, bConst], writes=[bP[4]])
            g.op("T", lambda e: e.matmul(ps[5][:, :], on1f[:, :], s1,
                                         start=True, stop=True),
                 reads=[bS1, bConst], writes=[bP[5]])
            for k in range(NDC):
                xv = x[:, k * TC:(k + 1) * TC]
                nv = xn[:, k * TC:(k + 1) * TC]
                g.op("V", (lambda xv_, nv_: lambda e: e.tensor_tensor(
                    nv_, xv_, ps[5][:, :], ALU.mult))(xv, nv),
                    reads=[bx[k], bP[5]], writes=[] if k else [bxn])
                g.op("V", (lambda nv_: lambda e: e.tensor_tensor(
                    nv_, nv_, ps[4][:, :], ALU.subtract))(nv),
                    reads=[bP[4]], writes=[bxn] if k == NDC - 1 else [])

        def ffn_head(c):
            x = xb[c % 2]
            bx = bX[c % 2]
            for e_i, bank in ((0, 2), (1, 3)):
                def lfn(k, e_i_, bank_):
                    return lambda e: e.matmul(
                        ps[bank_][0:1, :], wgf[:, k * 2 + e_i_:k * 2 + e_i_ + 1],
                        x[:, k * TC:(k + 1) * TC],
                        start=(k == 0), stop=(k == NDC - 1))
                g.op("T", [lfn(k, e_i, bank) for k in range(NDC)],
                     reads=list(bx) + [bConst], writes=[bP[bank]])
            g.op("S", lambda e: e.copy(s2v, ps[2][0:1, :]),
                 reads=[bP[2]], writes=[bS2])
            g.op("V", lambda e: e.tensor_tensor(s2v, s2v, ps[3][0:1, :],
                                                ALU.subtract),
                 reads=[bP[3]], writes=[bS2])
            g.op("V", lambda e: e.tensor_scalar(s1, s0, sdif[0:1, 0:1], None,
                                                ALU.mult),
                 reads=[bS0, bSdif], writes=[bS1, bSpB])
            g.op("V", lambda e: e.tensor_tensor(s2v, s2v, s1, ALU.subtract),
                 reads=[bS1], writes=[bS2])
            g.op("T", lambda e: e.matmul(ps[4][:, :], on1f[:, :], s2v,
                                         start=True, stop=True),
                 reads=[bS2, bConst], writes=[bP[4]])
            g.op("V", lambda e: e.tensor_single_scalar(mask_i[:, :], ps[4][:, :],
                                                       0.0, ALU.is_ge),
                 reads=[bP[4]], writes=[bMask])

        class WStream:
            def __init__(self):
                self.tiles = []
                self.cursor = 0

            def ensure(self, upto):
                while self.cursor < min(upto, len(self.tiles)):
                    i = self.cursor
                    s = i % 3
                    g.op("Y", dma(wr[s][:, :], self.tiles[i]), writes=[bWr[s]])
                    self.cursor += 1

        def ffn_pass(c, e_i, ws, tbase, pop, only_a=False):
            xn = xnb[c % 2]
            bxn = bXN[c % 2]
            bx = bX[c % 2]
            x = xb[c % 2]
            act = AF.Gelu if e_i == 0 else AF.Relu
            brow = (b0a if e_i == 0 else b1a)
            for mc in range(NFC):
                t_i = tbase + mc
                ws.ensure(t_i + 3)
                s_i = t_i % 3
                bank = gctr[0] % 4
                gctr[0] += 1
                def afn(k, s_i_, bank_):
                    return lambda e: e.matmul(
                        ps[bank_][:, :], wr[s_i_][:, k * 128:(k + 1) * 128],
                        xn[:, k * TC:(k + 1) * TC],
                        start=(k == 0), stop=(k == NDC - 1))
                g.op("T", [afn(k, s_i, bank) for k in range(NDC)],
                     reads=[bxn, bWr[s_i]], writes=[bP[bank]])
                g.op("S", (lambda mc_, bank_: lambda e: e.activation(
                    out=hbuf[:, mc_ * TC:(mc_ + 1) * TC], in_=ps[bank_][:, :],
                    func=act, bias=brow[:, mc_:mc_ + 1]))(mc, bank),
                    reads=[bP[bank], bConst],
                    writes=[bH] if mc in (0, NFC - 1) else [])
                pop()
            if only_a:
                return
            for mq in range(NDC):
                bank = gctr[0] % 4
                gctr[0] += 1
                t0 = tbase + NFC + 2 * mq
                ws.ensure(t0 + 3)
                sA, sB = t0 % 3, (t0 + 1) % 3
                def bfn(k, s_, bank_):
                    return lambda e: e.matmul(
                        ps[bank_][:, :], wr[s_][:, (k % 16) * 128:(k % 16 + 1) * 128],
                        hbuf[:, k * TC:(k + 1) * TC],
                        start=(k == 0), stop=False)
                g.op("T", [bfn(k, sA, bank) for k in range(16)],
                     reads=[bH, bWr[sA]], writes=[bP[bank]])
                def bias_mm(s_, bank_):
                    return lambda e: e.matmul(
                        ps[bank_][:, :], wr[s_][0:1, 2048:2176], onesb[:, :],
                        start=False, stop=True)
                g.op("T", [bfn(k, sB, bank) for k in range(16, 32)] +
                     [bias_mm(sB, bank)],
                     reads=[bH, bWr[sB], bConst], writes=[bP[bank]])
                if e_i == 0:
                    g.op("S", (lambda mq_, bank_: lambda e: e.copy(
                        out0[:, mq_ * TC:(mq_ + 1) * TC], ps[bank_][:, :]))(mq, bank),
                        reads=[bP[bank]],
                        writes=[bO0] if mq in (0, NDC - 1) else [])
                else:
                    while g.pending_prio:
                        g.pending_prio.pop(0)()
                    g.op("S", (lambda bank_: lambda e: e.copy(
                        out1[:, :], ps[bank_][:, :]))(bank),
                        reads=[bP[bank]], writes=[bO1])

                    def combine(mq_=mq, c_=c):
                        g.op("V", lambda e: e.copy_predicated(
                            out1[:, :], mask_i[:, :],
                            out0[:, mq_ * TC:(mq_ + 1) * TC]),
                            reads=[bMask, bO0], writes=[bO1])
                        g.op("V", lambda e: e.scalar_tensor_tensor(
                            x[:, mq_ * TC:(mq_ + 1) * TC], out1[:, :], 0.5,
                            x[:, mq_ * TC:(mq_ + 1) * TC], ALU.mult, ALU.add),
                            reads=[bO1], writes=[bx[mq_]])
                    g.push_prio(combine)
                pop()

        def ffn_phase(c, ws, tbase=None, pop=None):
            if getattr(ws, "pre", None) is not None:
                tbase = ws.pre
                ws.pre = None
            else:
                tbase = len(ws.tiles)
                ws.tiles.extend(wtiles_l[0])
            cnt = [0]

            def bpop():
                cnt[0] += 1
                if cnt[0] % 4 == 0:
                    g.pop_tasks(12)
                else:
                    g.pop_tasks(0)
            ffn_head(c)
            ffn_pass(c, 0, ws, tbase, bpop)
            ffn_pass(c, 1, ws, tbase + NFC + 2 * NDC, bpop)

        def mamba_prologue(c):
            xn = xnb[c % 2]
            bxn = bXN[c % 2]
            x = xb[c % 2]
            bx = bX[c % 2]
            for k in (range(NDC) if DBG_PHASE != "dl" else []):
                g.op("V", (lambda k_: lambda e: e.scalar_tensor_tensor(
                    x[:, k_ * TC:(k_ + 1) * TC], xn[:, k_ * TC:(k_ + 1) * TC],
                    dvec[:, k_:k_ + 1], x[:, k_ * TC:(k_ + 1) * TC],
                    ALU.mult, ALU.add))(k),
                    reads=[bxn, bConst], writes=[bx[k]])
            def drfn(k):
                return lambda e: e.matmul(
                    ps[0][:, :], xpw[:, k * 160:k * 160 + 128],
                    xn[:, k * TC:(k + 1) * TC], start=(k == 0), stop=(k == NDC - 1))
            def bcfn(k):
                return lambda e: e.matmul(
                    ps[1][0:32, :], xpw[:, k * 160 + 128:k * 160 + 160],
                    xn[:, k * TC:(k + 1) * TC], start=(k == 0), stop=(k == NDC - 1))
            mms = []
            for k in range(NDC):
                mms.append(drfn(k))
                mms.append(bcfn(k))
            g.op("T", mms, reads=[bxn, bConst], writes=[bP[0], bP[1]])
            g.op("S", lambda e: e.copy(drs[:, :], ps[0][:, :]),
                 reads=[bP[0]], writes=[bDrs])
            g.op("S", lambda e: e.copy(bcs[:, :], ps[1][0:32, :]),
                 reads=[bP[1]], writes=[bBcs])
            for q in range(NQ):
                for j in range(GN):
                    g.push_unit(make_burst(c, q, j))
                for dc in range(NDC):
                    g.push_unit(make_delta(c, q, dc))
                    for j in range(GN):
                        g.push_unit(make_unit(c, q, dc, j))

        def make_burst(c, q, j):
            def burst():
                n = q * GN + j
                for row, dstbuf, dst in ((n, bBbh, Bbh), (16 + n, bCbh, Cbh)):
                    g.op("T", (lambda row_: lambda e: e.matmul(
                        ps[6][0:1, :], selc[0:32, row_:row_ + 1], bcs[:, :],
                        start=True, stop=True))(row),
                        reads=[bBcs, bConst], writes=[bP[6]])
                    g.op("S", lambda e: e.copy(stg[:, :], ps[6][0:1, :]),
                         reads=[bP[6]], writes=[bO1])
                    g.op("T", lambda e: e.matmul(ps[7][:, :], on1b[:, :], stg[:, :],
                                                 start=True, stop=True),
                         reads=[bO1, bConst], writes=[bP[7]])
                    g.op("S", (lambda dst_, j_: lambda e: e.copy(
                        dst_[:, j_ * TC:(j_ + 1) * TC], ps[7][:, :]))(dst, j),
                        reads=[bP[7]], writes=[dstbuf])
            return burst

        def make_delta(c, q, dc):
            def delta():
                bank = 6 if dc % 2 == 0 else 7
                g.op("T", (lambda bank_, dc_: lambda e: e.matmul(
                    ps[bank_][:, :], dtw[:, dc_ * 128:(dc_ + 1) * 128], drs[:, :],
                    start=True, stop=True))(bank, dc),
                    reads=[bDrs, bConst], writes=[bP[bank]])
                g.op("S", (lambda bank_, dc_: lambda e: e.activation(
                    out=sp_a[:, :], in_=ps[bank_][:, :], func=AF.Relu,
                    bias=dtb[:, dc_:dc_ + 1]))(bank, dc),
                    reads=[bP[bank], bConst], writes=[bSpA, bS0])
                g.op("S", (lambda bank_, dc_: lambda e: e.activation(
                    out=sp_b[:, :], in_=ps[bank_][:, :], func=AF.Abs,
                    bias=dtb[:, dc_:dc_ + 1]))(bank, dc),
                    reads=[bP[bank], bConst], writes=[bSpB, bS1])
                g.op("S", lambda e: e.activation(out=delta_t[:, :], in_=sp_b[:, :],
                                                 func=AF.Exp, scale=-1.0),
                     reads=[bSpB], writes=[bDelta])
                g.op("S", lambda e: e.activation(out=sp_b[:, :], in_=delta_t[:, :],
                                                 func=AF.Ln, bias=1.0),
                     reads=[bDelta, bConst], writes=[bSpB, bS1])
                g.op("V", lambda e: e.tensor_tensor(delta_t[:, :], sp_a[:, :],
                                                    sp_b[:, :], ALU.add),
                     reads=[bSpA, bSpB], writes=[bDelta])
                xn = xnb[c % 2]
                g.op("V", (lambda dc_: lambda e: e.tensor_tensor(
                    du_t[:, :], delta_t[:, :], xn[:, dc_ * TC:(dc_ + 1) * TC],
                    ALU.mult))(dc),
                    reads=[bDelta, bXN[c % 2]], writes=[bDu])
                if DBG_PHASE == "dl" and q == 0:
                    x = xb[c % 2]
                    g.op("V", (lambda dc_: lambda e: e.tensor_copy(
                        x[:, dc_ * TC:(dc_ + 1) * TC], delta_t[:, :]))(dc),
                        reads=[bDelta], writes=[bX[c % 2][dc]])
            return delta

        def make_unit(c, q, dc, j):
            def unit():
                U = unit_ctr[0]
                unit_ctr[0] += 1
                n = q * GN + j
                idx = dc * NST + n
                r = U % 2
                ra = U % RDA
                g.op("S", (lambda ra_, idx_: lambda e: e.activation(
                    out=dA_r[ra_][:, :], in_=delta_t[:, :], func=AF.Exp,
                    scale=negA[:, idx_:idx_ + 1]))(ra, idx),
                    reads=[bDelta, bPosA], writes=[bDA[ra]])
                g.op("V", (lambda r_, j_: lambda e: e.tensor_tensor(
                    dBu_r[r_][:, :], du_t[:, :], Bbh[:, j_ * TC:(j_ + 1) * TC],
                    ALU.mult))(r, j),
                    reads=[bDu, bBbh], writes=[bDBu[r]])
                carry = savepos.get(U - NDC * NST)
                if carry:
                    g.p.wait("V", "V", carry)
                else:
                    for s_, v_ in bHst.wr.d.items():
                        g.p.wait("V", s_, v_)
                g.op("V", (lambda r_, ra_, idx_: lambda e: e.tensor_tensor_scan(
                    h_r[r_][:, :], dA_r[ra_][:, :], dBu_r[r_][:, :],
                    h_state[:, idx_:idx_ + 1], ALU.mult, ALU.add))(r, ra, idx),
                    reads=[bDA[ra], bDBu[r]], writes=[bHr[r]])
                p = g.op("V", (lambda r_, idx_: lambda e: e.tensor_copy(
                    h_state[:, idx_:idx_ + 1], h_r[r_][:, TC - 1:TC]))(r, idx),
                    reads=[bHr[r]])
                savepos[U] = p
                if DBG_PHASE == "dl":
                    return
                x = xb[c % 2]
                dst = 0 if j == 0 else 1
                g.op("G", (lambda r_, j_, dst_: lambda e: e.tensor_tensor(
                    ymt[dst_][:, :], h_r[r_][:, :], Cbh[:, j_ * TC:(j_ + 1) * TC],
                    ALU.mult))(r, j, dst),
                    reads=[bHr[r], bCbh], writes=[bYmt[dst]])
                if j > 0:
                    g.op("G", lambda e: e.tensor_tensor(
                        ymt[0][:, :], ymt[0][:, :], ymt[1][:, :], ALU.add),
                        reads=[bYmt[1]], writes=[bYmt[0]])
                if j == GN - 1:
                    g.op("G", (lambda dc_: lambda e: e.tensor_tensor(
                        x[:, dc_ * TC:(dc_ + 1) * TC], x[:, dc_ * TC:(dc_ + 1) * TC],
                        ymt[0][:, :], ALU.add))(dc),
                        reads=[bYmt[0]], writes=[bX[c % 2][dc]])
            return unit

        load_x(0)
        load_x(1)
        wtiles = []
        for e_i in range(2):
            for mc in range(32):
                wtiles.append(wah[e_i][mc])
            for mq in range(16):
                wtiles.append(wbh[e_i][mq, 0])
                wtiles.append(wbh[e_i][mq, 1])
        wtiles_l = [wtiles]
        gws = WStream()
        if DBG_PHASE is not None:
            for c in range(NCH):
                layernorm(c)
                if DBG_PHASE == "ln":
                    xd, xnd = xb[c % 2], xnb[c % 2]
                    for k in range(NDC):
                        g.op("V", (lambda k_, xd_, xnd_: lambda e: e.tensor_copy(
                            xd_[:, k_ * TC:(k_ + 1) * TC],
                            xnd_[:, k_ * TC:(k_ + 1) * TC]))(k, xd, xnd),
                            reads=[bXN[c % 2]], writes=[bX[c % 2][k]])
                if DBG_PHASE == "ffa":
                    ws = WStream(); ws.tiles = wtiles
                    ffn_pass(c, 0, ws, 0, lambda: g.pop_tasks(2), only_a=True)
                    xd = xb[c % 2]
                    for k in range(NDC):
                        g.op("V", (lambda k_, xd_: lambda e: e.tensor_copy(
                            xd_[:, k_ * TC:(k_ + 1) * TC],
                            hbuf[:, k_ * TC:(k_ + 1) * TC]))(k, xd),
                            reads=[bH], writes=[bX[c % 2][k]])
                if DBG_PHASE == "ffb":
                    ws = WStream(); ws.tiles = wtiles
                    ffn_pass(c, 0, ws, 0, lambda: g.pop_tasks(2))
                    xd = xb[c % 2]
                    for k in range(NDC):
                        g.op("V", (lambda k_, xd_: lambda e: e.tensor_copy(
                            xd_[:, k_ * TC:(k_ + 1) * TC],
                            out0[:, k_ * TC:(k_ + 1) * TC]))(k, xd),
                            reads=[bO0], writes=[bX[c % 2][k]])
                if DBG_PHASE not in ("id", "ln", "ffa", "ffb"):
                    ws = WStream()
                    ffn_phase(c, ws)
                    g.flush()
                if DBG_PHASE in ("mamba", "ffn2", "dl"):
                    layernorm(c)
                    mamba_prologue(c)
                    g.flush()
                if DBG_PHASE == "ffn2":
                    layernorm(c)
                    ws = WStream()
                    ffn_phase(c, ws)
                    g.flush()
                store_x(c)
                if c + 2 < NCH:
                    load_x(c + 2)
            g.p.ops["Y"].append(("w", "O", g.p.ct["O"]))
        for it in (range(NCH + 1) if DBG_PHASE is None else []):
            if it < NCH:
                c = it
                layernorm(c)
                ffn_phase(c, gws)
                g.flush()
            if it >= 1:
                g.flush()
                layernorm(it - 1)
            if it < NCH:
                layernorm(c)
                gws.tiles.extend(wtiles)
                gws.pre = len(gws.tiles) - len(wtiles)
                gws.ensure(gws.pre + 3)
                mamba_prologue(c)
            if it >= 1:
                cm = it - 1
                ffn_phase(cm, gws)
                store_x(cm)
                if cm + 2 < NCH:
                    load_x(cm + 2)
        g.p.ops["Y"].append(("w", "O", g.p.ct["O"]))

        def replay(name):
            def run(eng):
                for item in g.p.ops[name]:
                    if item[0] == "w":
                        eng.wait_ge(sems[item[1]], item[2])
                    else:
                        inst = item[1](eng)
                        if item[2] is not None:
                            inst.then_inc(sems[item[2]], item[3])
            return run

        @block.tensor
        def _(tensor):
            replay("T")(tensor)

        @block.scalar
        def _(scalar):
            replay("S")(scalar)

        @block.vector
        def _(vector):
            replay("V")(vector)

        @block.gpsimd
        def _(gpsimd):
            replay("G")(gpsimd)

        @block.sync
        def _(sync):
            replay("Y")(sync)

    return nc


def _prep_weights(i):
    out = {}
    for e, (wa_k, ba_k, wb_k, bb_k) in enumerate(
            [("L0_ffn_W0a", "L0_ffn_b0a", "L0_ffn_W0b", "L0_ffn_b0b"),
             ("L0_ffn_W1a", "L0_ffn_b1a", "L0_ffn_W1b", "L0_ffn_b1b")]):
        wa = np.asarray(i[wa_k], dtype=np.float32)
        wb = np.asarray(i[wb_k], dtype=np.float32)
        bb = np.asarray(i[bb_k], dtype=np.float32)
        wah = np.zeros((32, 128, WCOL), dtype=BF)
        wah[:, :, :2048] = wa.reshape(32, 128, 16, 128).transpose(
            0, 3, 2, 1).reshape(32, 128, 2048).astype(BF)
        wbh = np.zeros((16, 2, 128, WCOL), dtype=BF)
        wbt = wb.reshape(16, 128, 32, 128).transpose(0, 2, 3, 1)
        wbh[:, 0, :, :2048] = wbt[:, 0:16].transpose(0, 2, 1, 3).reshape(
            16, 128, 2048).astype(BF)
        wbh[:, 1, :, :2048] = wbt[:, 16:32].transpose(0, 2, 1, 3).reshape(
            16, 128, 2048).astype(BF)
        wbh[:, 1, 0, 2048:2176] = bb.reshape(16, 128).astype(BF)
        out[f"wa{e}"] = wah
        out[f"wb{e}"] = wbh
        out[f"b{e}a_h"] = np.ascontiguousarray(
            np.asarray(i[ba_k], dtype=np.float32).reshape(32, 128).T)
    xpw = np.asarray(i["L0_attn_x_proj_w"], dtype=np.float32)
    out["xpw_h"] = np.ascontiguousarray(
        xpw.reshape(160, 16, 128).transpose(2, 1, 0)).reshape(128, 16 * 160).astype(BF)
    dtw = np.asarray(i["L0_attn_dt_proj_w"], dtype=np.float32)
    out["dtw_h"] = np.ascontiguousarray(
        dtw.reshape(16, 128, 128).transpose(2, 0, 1)).reshape(128, 2048).astype(BF)
    wgm = np.asarray(i["L0_ffn_Wg"], dtype=np.float32)
    out["wgf_h"] = np.ascontiguousarray(
        wgm.reshape(2, 16, 128).transpose(2, 1, 0).reshape(128, 32))
    selc = np.zeros((32, 32), dtype=BF)
    for r in range(32):
        selc[r, r] = 1.0
    out["selc_h"] = selc
    out["alog_h"] = np.ascontiguousarray(
        np.asarray(i["L0_attn_A_log"], dtype=np.float32).reshape(
            16, 128, 16).transpose(1, 0, 2)).reshape(128, 256)
    out["dvec_h"] = np.ascontiguousarray(
        np.asarray(i["L0_attn_D"], dtype=np.float32).reshape(16, 128).T)
    out["dtb_h"] = np.ascontiguousarray(
        np.asarray(i["L0_attn_dt_proj_b"], dtype=np.float32).reshape(16, 128).T)
    out["onesb_h"] = np.ones((1, TC), dtype=BF)
    out["on1f_h"] = np.ones((1, 128), dtype=np.float32)
    out["onesk_h"] = np.ones((128, 1), dtype=np.float32)
    return out


def kernel(**inputs):
    shared = _prep_weights(inputs)
    x = np.asarray(inputs["L0_attn_x"], dtype=np.float32)
    in_maps = []
    for b in range(8):
        m = dict(shared)
        m["xT"] = np.ascontiguousarray(x[b].T)
        in_maps.append(m)
    nc = build_program()
    res = run_bass_kernel_spmd(nc, in_maps, core_ids=list(range(8)))
    outs = [np.ascontiguousarray(np.asarray(r["yT"], dtype=np.float32).T)
            for r in res.results]
    return np.stack(outs, axis=0)


# version 25
# speedup vs baseline: 1.0003x; 1.0003x over previous
import contextlib
import numpy as np
import ml_dtypes
import concourse.bass as bass
import concourse.mybir as mybir
from concourse.bass_utils import run_bass_kernel_spmd

F32 = mybir.dt.float32
BF16 = mybir.dt.bfloat16
U8 = mybir.dt.uint8
AF = mybir.ActivationFunctionType
ALU = mybir.AluOpType
BF = ml_dtypes.bfloat16

DBG_PHASE = None

DM = 2048
SEQ = 4096
TC = 512
NCH = SEQ // TC
NDC = DM // 128
NFC = 4096 // 128
NST = 16
NQ = 4
GN = NST // NQ
WCOL = 2176
EPS = 1e-5


class Dep:
    __slots__ = ("d",)

    def __init__(self):
        self.d = {}

    def add(self, sem, val):
        if val and self.d.get(sem, 0) < val:
            self.d[sem] = val


class Buf:
    __slots__ = ("rd", "wr")

    def __init__(self):
        self.rd = Dep()
        self.wr = Dep()


class Prog:
    def __init__(self):
        self.ops = {e: [] for e in "TSVGY"}
        self.ct = {"T": 0, "S": 0, "V": 0, "G": 0, "D": 0, "O": 0, "C": 0}
        self.waited = {e: {} for e in "TSVGY"}

    def wait(self, eng, sem, val):
        if val <= 0:
            return
        w = self.waited[eng]
        if w.get(sem, 0) >= val:
            return
        w[sem] = val
        self.ops[eng].append(("w", sem, val))

    def raw_emit(self, eng, fn, inc, amt):
        pos = None
        if inc is not None:
            self.ct[inc] += amt
            pos = self.ct[inc]
        self.ops[eng].append(("i", fn, inc, amt))
        return pos


ESEM = {"T": "T", "S": "S", "V": "V", "G": "G"}


class Gen:

    def __init__(self):
        self.p = Prog()
        self.pending_units = []
        self.pending_prio = []

    def op(self, eng, fns, reads=(), writes=(), inc=True, amt=1, sem=None):
        p = self.p
        dep = Dep()
        for b in reads:
            for s, v in b.wr.d.items():
                dep.add(s, v)
        for b in writes:
            for s, v in b.rd.d.items():
                dep.add(s, v)
            for s, v in b.wr.d.items():
                dep.add(s, v)
        for s, v in sorted(dep.d.items()):
            p.wait(eng, s, v)
        if not isinstance(fns, (list, tuple)):
            fns = [fns]
        pos = None
        if eng == "Y":
            for fn in fns[:-1]:
                p.raw_emit(eng, fn, None, 0)
            pos = p.raw_emit(eng, fns[-1], sem or "D", 16)
            s_used = sem or "D"
        else:
            s_used = ESEM[eng]
            for fn in fns[:-1]:
                p.raw_emit(eng, fn, None, 0)
            if inc:
                pos = p.raw_emit(eng, fns[-1], s_used, amt)
            else:
                p.raw_emit(eng, fns[-1], None, 0)
        for b in reads:
            if pos:
                b.rd.add(s_used, pos)
        for b in writes:
            if pos:
                b.wr = Dep()
                b.wr.add(s_used, pos)
                b.rd = Dep()
        return pos

    def push_unit(self, fn):
        self.pending_units.append(fn)

    def push_prio(self, fn):
        self.pending_prio.append(fn)

    def pop_tasks(self, k):
        n = 0
        while self.pending_prio and n < k:
            self.pending_prio.pop(0)()
            n += 1
        while self.pending_units and n < k:
            self.pending_units.pop(0)()
            n += 1

    def flush(self):
        while self.pending_prio:
            self.pending_prio.pop(0)()
        while self.pending_units:
            self.pending_units.pop(0)()


def build_program():
    nc = bass.Bass("TRN2", target_bir_lowering=False)

    xT = nc.dram_tensor("xT", [DM, SEQ], F32, kind="ExternalInput").ap()
    yT = nc.dram_tensor("yT", [DM, SEQ], F32, kind="ExternalOutput").ap()
    wah = [nc.dram_tensor(f"wa{e}", [32, 128, WCOL], BF16, kind="ExternalInput").ap()
           for e in range(2)]
    wbh = [nc.dram_tensor(f"wb{e}", [16, 2, 128, WCOL], BF16, kind="ExternalInput").ap()
           for e in range(2)]
    xpw_h = nc.dram_tensor("xpw_h", [128, NDC * 160], BF16, kind="ExternalInput").ap()
    dtw_h = nc.dram_tensor("dtw_h", [128, DM], BF16, kind="ExternalInput").ap()
    wgf_h = nc.dram_tensor("wgf_h", [128, NDC * 2], F32, kind="ExternalInput").ap()
    selc_h = nc.dram_tensor("selc_h", [32, 32], BF16, kind="ExternalInput").ap()
    alog_h = nc.dram_tensor("alog_h", [128, NDC * NST], F32, kind="ExternalInput").ap()
    dvec_h = nc.dram_tensor("dvec_h", [128, NDC], F32, kind="ExternalInput").ap()
    dtb_h = nc.dram_tensor("dtb_h", [128, NDC], F32, kind="ExternalInput").ap()
    b0a_h = nc.dram_tensor("b0a_h", [128, NFC], F32, kind="ExternalInput").ap()
    b1a_h = nc.dram_tensor("b1a_h", [128, NFC], F32, kind="ExternalInput").ap()
    onesb_h = nc.dram_tensor("onesb_h", [1, TC], BF16, kind="ExternalInput").ap()
    on1f_h = nc.dram_tensor("on1f_h", [1, 128], F32, kind="ExternalInput").ap()
    onesk_h = nc.dram_tensor("onesk_h", [128, 1], F32, kind="ExternalInput").ap()

    with contextlib.ExitStack() as st:
        def sb(name, shape, dt=F32):
            return st.enter_context(nc.sbuf_tensor(name, shape, dt))

        xb = [sb("xb0", [128, NDC * TC]), sb("xb1", [128, NDC * TC])]
        xnb = [sb("xnb0", [128, NDC * TC], BF16), sb("xnb1", [128, NDC * TC], BF16)]
        hbuf = sb("hbuf", [128, NFC * TC], BF16)
        out0 = sb("out0", [128, NDC * TC], BF16)
        out1 = sb("out1", [128, TC], BF16)
        wr = [sb(f"wr{i}", [128, WCOL], BF16) for i in range(3)]
        xpw = sb("xpw", [128, NDC * 160], BF16)
        dtw = sb("dtw", [128, DM], BF16)
        wgf = sb("wgf", [128, NDC * 2])
        sdif = sb("sdif", [1, 1])
        selc = sb("selc", [32, 32], BF16)
        negA = sb("negA", [128, NDC * NST])
        dvec = sb("dvec", [128, NDC])
        dtb = sb("dtb", [128, NDC])
        b0a = sb("b0a", [128, NFC])
        b1a = sb("b1a", [128, NFC])
        onesb = sb("onesb", [1, TC], BF16)
        on1b = onesb[0:1, 0:128]
        on1f = sb("on1f", [1, 128])
        onesk = sb("onesk", [128, 1])
        drs = sb("drs", [128, TC], BF16)
        bcs = sb("bcs", [32, TC], BF16)
        Bbh = sb("Bbh", [128, GN * TC], BF16)
        Cbh = sb("Cbh", [128, GN * TC], BF16)
        stg = out1[0:1, :]
        delta_t = sb("delta_t", [128, TC])
        du_t = sb("du_t", [128, TC], BF16)
        RDA = 3
        dA_r = [sb(f"dA{i}", [128, TC]) for i in range(RDA)]
        dBu_r = [sb("dBu0", [128, TC], BF16), sb("dBu1", [128, TC], BF16)]
        h_r = [sb("h0", [128, TC], BF16), sb("h1", [128, TC], BF16)]
        ymt = [sb("ymt0", [128, TC], BF16), sb("ymt1", [128, TC], BF16)]
        h_state = sb("h_state", [128, NDC * NST])
        sp_a = sb("sp_a", [128, TC])
        sp_b = sb("sp_b", [128, TC])
        sq = [sb("sq0", [128, TC], BF16), sb("sq1", [128, TC], BF16)]
        st2 = sb("st2", [1, TC])
        onesk_bf = sb("onesk_bf", [128, 1], BF16)
        mask_i = sb("mask_i", [128, TC], U8)

        ps = [st.enter_context(nc.psum_tensor(f"p{i}", [128, TC], F32))
              for i in range(8)]

        if globals().get("PROBE_SBUF"):
            lo, hi = 0, 300000
            while lo < hi - 64:
                mid = (lo + hi) // 2
                try:
                    cm = nc.sbuf_tensor("probe_x", [128, mid], U8)
                    cm.__enter__()
                    cm.__exit__(None, None, None)
                    lo = mid
                except Exception:
                    hi = mid
            print(f"SBUF headroom: {lo} bytes/partition")

        sems = {}
        for s in ("T", "S", "V", "G", "D", "O", "C"):
            sems[s] = st.enter_context(nc.semaphore(f"sem{s}"))
        block = st.enter_context(nc.Block())

        s0 = sp_a[0:1, :]
        s1 = sp_b[0:1, :]
        s2v = st2[0:1, :]

        g = Gen()

        bX = [[Buf() for _ in range(NDC)], [Buf() for _ in range(NDC)]]
        bXN = [Buf(), Buf()]
        bH = Buf()
        bO0 = Buf()
        bO1 = Buf()
        bWr = [Buf(), Buf(), Buf()]
        bP = [Buf() for _ in range(8)]
        bDrs, bBcs = Buf(), Buf()
        bBbh, bCbh = Buf(), Buf()
        bDelta, bDu = Buf(), Buf()
        bDA = [Buf() for _ in range(RDA)]
        bDBu = [Buf(), Buf()]
        bHr = [Buf(), Buf()]
        bYmt = [Buf(), Buf()]
        bSpA = Buf()
        bSpB = Buf()
        bSq = [Buf(), Buf()]
        bS0, bS1, bS2 = Buf(), Buf(), Buf()
        bMask = Buf()
        bConst = Buf()
        bPosA, bNdtb = Buf(), Buf()
        bHst = Buf()

        savepos = {}
        unit_ctr = [0]
        gctr = [0]

        def dma(dst_ap, src_ap):
            return lambda e: e.dma_start(out=dst_ap, in_=src_ap)

        const_loads = [
            (xpw[:, :], xpw_h), (dtw[:, :], dtw_h), (wgf[:, :], wgf_h),
            (selc[:, :], selc_h), (negA[:, :], alog_h), (dvec[:, :], dvec_h),
            (dtb[:, :], dtb_h), (b0a[:, :], b0a_h), (b1a[:, :], b1a_h),
            (onesb[:, :], onesb_h), (on1f[:, :], on1f_h),
            (onesk[:, :], onesk_h),
        ]
        for d, s in const_loads:
            g.op("Y", dma(d, s), writes=[bConst], sem="C")
        g.op("S", lambda e: e.activation(out=sp_a[:, 0:NDC * NST], in_=negA[:, :],
                                         func=AF.Exp),
             reads=[bConst], writes=[bSpA])
        g.op("S", lambda e: e.mul(negA[:, :], sp_a[:, 0:NDC * NST], -1.0),
             reads=[bSpA], writes=[bPosA])
        g.op("V", lambda e: e.memset(h_state[:, :], 0.0), writes=[bHst])
        g.op("V", lambda e: e.tensor_copy(onesk_bf[:, :], onesk[:, :]),
             reads=[bConst], writes=[bConst])
        nc.const_aps.register(F32, 1.0, onesk[:, :])
        bSdif = Buf()
        for e_i, bank in ((0, 2), (1, 3)):
            def sfn(k, e_i_, bank_):
                return lambda e: e.matmul(
                    ps[bank_][0:1, 0:1], wgf[:, k * 2 + e_i_:k * 2 + e_i_ + 1],
                    onesk[:, :], start=(k == 0), stop=(k == NDC - 1))
            g.op("T", [sfn(k, e_i, bank) for k in range(NDC)],
                 reads=[bConst], writes=[bP[bank]])
        g.op("S", lambda e: e.copy(st2[0:1, 0:1], ps[2][0:1, 0:1]),
             reads=[bP[2]], writes=[bS2])
        g.op("V", lambda e: e.tensor_tensor(sdif[:, :], st2[0:1, 0:1],
                                            ps[3][0:1, 0:1], ALU.subtract),
             reads=[bS2, bP[3]], writes=[bSdif])

        xT3 = xT.rearrange("(dc p) t -> p dc t", p=128)
        yT3 = yT.rearrange("(dc p) t -> p dc t", p=128)

        def load_x(c):
            buf = xb[c % 2]
            b3 = buf.rearrange("p (dc t) -> p dc t", dc=NDC)
            for gq in range(4):
                g.op("Y", dma(b3[:, gq * 4:(gq + 1) * 4, :],
                              xT3[:, gq * 4:(gq + 1) * 4, c * TC:(c + 1) * TC]),
                     writes=bX[c % 2][gq * 4:(gq + 1) * 4])

        def store_x(c):
            while g.pending_prio:
                g.pending_prio.pop(0)()
            buf = xb[c % 2]
            b3 = buf.rearrange("p (dc t) -> p dc t", dc=NDC)
            for gq in range(4):
                g.op("Y", dma(yT3[:, gq * 4:(gq + 1) * 4, c * TC:(c + 1) * TC],
                              b3[:, gq * 4:(gq + 1) * 4, :]),
                     reads=bX[c % 2][gq * 4:(gq + 1) * 4], sem="O")

        def layernorm(c):
            x = xb[c % 2]
            xn = xnb[c % 2]
            bx, bxn = bX[c % 2], bXN[c % 2]
            def s1fn(k):
                return lambda e: e.matmul(ps[2][0:1, :], onesk[:, :],
                                          x[:, k * TC:(k + 1) * TC],
                                          start=(k == 0), stop=(k == NDC - 1))
            for k in range(NDC):
                g.op("T", s1fn(k), reads=[bx[k], bConst], writes=[bP[2]])
            for k in range(NDC):
                sl = sq[k % 2]
                g.op("S", (lambda sl_, k_: lambda e: e.activation(
                    out=sl_[:, :], in_=x[:, k_ * TC:(k_ + 1) * TC], func=AF.Square))(sl, k),
                    reads=[bx[k]], writes=[bSq[k % 2]])
                g.op("T", (lambda sl_, k_: lambda e: e.matmul(
                    ps[3][0:1, :], onesk_bf[:, :], sl_[:, :],
                    start=(k_ == 0), stop=(k_ == NDC - 1)))(sl, k),
                    reads=[bSq[k % 2], bConst], writes=[bP[3]])
            g.op("V", lambda e: e.tensor_scalar(s0, ps[2][0:1, :], 1.0 / DM,
                                                None, ALU.mult),
                 reads=[bP[2]], writes=[bS0, bSpA])
            g.op("V", lambda e: e.tensor_scalar(s1, ps[3][0:1, :], 1.0 / DM,
                                                None, ALU.mult),
                 reads=[bP[3]], writes=[bS1, bSpB])
            g.op("V", lambda e: e.tensor_tensor(s2v, s0, s0, ALU.mult),
                 reads=[bS0], writes=[bS2])
            g.op("V", lambda e: e.tensor_tensor(s1, s1, s2v, ALU.subtract),
                 reads=[bS2], writes=[bS1, bSpB])
            g.op("V", lambda e: e.tensor_scalar(s1, s1, EPS, None, ALU.add),
                 writes=[bS1, bSpB])
            g.op("S", lambda e: e.activation(out=s2v, in_=s1, func=AF.Ln),
                 reads=[bS1], writes=[bS2])
            g.op("S", lambda e: e.activation(out=s1, in_=s2v, func=AF.Exp,
                                             scale=-0.5),
                 reads=[bS2], writes=[bS1, bSpB])
            g.op("V", lambda e: e.tensor_tensor(s2v, s0, s1, ALU.mult),
                 reads=[bS0, bS1], writes=[bS2])
            g.op("T", lambda e: e.matmul(ps[4][:, :], on1f[:, :], s2v,
                                         start=True, stop=True),
                 reads=[bS2, bConst], writes=[bP[4]])
            g.op("T", lambda e: e.matmul(ps[5][:, :], on1f[:, :], s1,
                                         start=True, stop=True),
                 reads=[bS1, bConst], writes=[bP[5]])
            for k in range(NDC):
                xv = x[:, k * TC:(k + 1) * TC]
                nv = xn[:, k * TC:(k + 1) * TC]
                g.op("V", (lambda xv_, nv_: lambda e: e.tensor_tensor(
                    nv_, xv_, ps[5][:, :], ALU.mult))(xv, nv),
                    reads=[bx[k], bP[5]], writes=[] if k else [bxn])
                g.op("V", (lambda nv_: lambda e: e.tensor_tensor(
                    nv_, nv_, ps[4][:, :], ALU.subtract))(nv),
                    reads=[bP[4]], writes=[bxn] if k == NDC - 1 else [])

        def ffn_head(c):
            x = xb[c % 2]
            bx = bX[c % 2]
            for e_i, bank in ((0, 2), (1, 3)):
                def lfn(k, e_i_, bank_):
                    return lambda e: e.matmul(
                        ps[bank_][0:1, :], wgf[:, k * 2 + e_i_:k * 2 + e_i_ + 1],
                        x[:, k * TC:(k + 1) * TC],
                        start=(k == 0), stop=(k == NDC - 1))
                g.op("T", [lfn(k, e_i, bank) for k in range(NDC)],
                     reads=list(bx) + [bConst], writes=[bP[bank]])
            g.op("S", lambda e: e.copy(s2v, ps[2][0:1, :]),
                 reads=[bP[2]], writes=[bS2])
            g.op("V", lambda e: e.tensor_tensor(s2v, s2v, ps[3][0:1, :],
                                                ALU.subtract),
                 reads=[bP[3]], writes=[bS2])
            g.op("V", lambda e: e.tensor_scalar(s1, s0, sdif[0:1, 0:1], None,
                                                ALU.mult),
                 reads=[bS0, bSdif], writes=[bS1, bSpB])
            g.op("V", lambda e: e.tensor_tensor(s2v, s2v, s1, ALU.subtract),
                 reads=[bS1], writes=[bS2])
            g.op("T", lambda e: e.matmul(ps[4][:, :], on1f[:, :], s2v,
                                         start=True, stop=True),
                 reads=[bS2, bConst], writes=[bP[4]])
            g.op("V", lambda e: e.tensor_single_scalar(mask_i[:, :], ps[4][:, :],
                                                       0.0, ALU.is_ge),
                 reads=[bP[4]], writes=[bMask])

        class WStream:
            def __init__(self):
                self.tiles = []
                self.cursor = 0

            def ensure(self, upto):
                while self.cursor < min(upto, len(self.tiles)):
                    i = self.cursor
                    s = i % 3
                    g.op("Y", dma(wr[s][:, :], self.tiles[i]), writes=[bWr[s]])
                    self.cursor += 1

        def ffn_pass(c, e_i, ws, tbase, pop, only_a=False):
            xn = xnb[c % 2]
            bxn = bXN[c % 2]
            bx = bX[c % 2]
            x = xb[c % 2]
            act = AF.Gelu if e_i == 0 else AF.Relu
            brow = (b0a if e_i == 0 else b1a)
            for mc in range(NFC):
                t_i = tbase + mc
                ws.ensure(t_i + 3)
                s_i = t_i % 3
                bank = gctr[0] % 4
                gctr[0] += 1
                def afn(k, s_i_, bank_):
                    return lambda e: e.matmul(
                        ps[bank_][:, :], wr[s_i_][:, k * 128:(k + 1) * 128],
                        xn[:, k * TC:(k + 1) * TC],
                        start=(k == 0), stop=(k == NDC - 1))
                g.op("T", [afn(k, s_i, bank) for k in range(NDC)],
                     reads=[bxn, bWr[s_i]], writes=[bP[bank]])
                g.op("S", (lambda mc_, bank_: lambda e: e.activation(
                    out=hbuf[:, mc_ * TC:(mc_ + 1) * TC], in_=ps[bank_][:, :],
                    func=act, bias=brow[:, mc_:mc_ + 1]))(mc, bank),
                    reads=[bP[bank], bConst],
                    writes=[bH] if mc in (0, NFC - 1) else [])
                pop()
            if only_a:
                return
            for mq in range(NDC):
                bank = gctr[0] % 4
                gctr[0] += 1
                t0 = tbase + NFC + 2 * mq
                ws.ensure(t0 + 3)
                sA, sB = t0 % 3, (t0 + 1) % 3
                def bfn(k, s_, bank_):
                    return lambda e: e.matmul(
                        ps[bank_][:, :], wr[s_][:, (k % 16) * 128:(k % 16 + 1) * 128],
                        hbuf[:, k * TC:(k + 1) * TC],
                        start=(k == 0), stop=False)
                g.op("T", [bfn(k, sA, bank) for k in range(16)],
                     reads=[bH, bWr[sA]], writes=[bP[bank]])
                def bias_mm(s_, bank_):
                    return lambda e: e.matmul(
                        ps[bank_][:, :], wr[s_][0:1, 2048:2176], onesb[:, :],
                        start=False, stop=True)
                g.op("T", [bfn(k, sB, bank) for k in range(16, 32)] +
                     [bias_mm(sB, bank)],
                     reads=[bH, bWr[sB], bConst], writes=[bP[bank]])
                if e_i == 0:
                    g.op("S", (lambda mq_, bank_: lambda e: e.copy(
                        out0[:, mq_ * TC:(mq_ + 1) * TC], ps[bank_][:, :]))(mq, bank),
                        reads=[bP[bank]],
                        writes=[bO0] if mq in (0, NDC - 1) else [])
                else:
                    while g.pending_prio:
                        g.pending_prio.pop(0)()
                    g.op("S", (lambda bank_: lambda e: e.copy(
                        out1[:, :], ps[bank_][:, :]))(bank),
                        reads=[bP[bank]], writes=[bO1])

                    def combine(mq_=mq, c_=c):
                        g.op("V", lambda e: e.copy_predicated(
                            out1[:, :], mask_i[:, :],
                            out0[:, mq_ * TC:(mq_ + 1) * TC]),
                            reads=[bMask, bO0], writes=[bO1])
                        g.op("V", lambda e: e.scalar_tensor_tensor(
                            x[:, mq_ * TC:(mq_ + 1) * TC], out1[:, :], 0.5,
                            x[:, mq_ * TC:(mq_ + 1) * TC], ALU.mult, ALU.add),
                            reads=[bO1], writes=[bx[mq_]])
                    g.push_prio(combine)
                pop()

        def ffn_phase(c, ws, tbase=None, pop=None):
            if getattr(ws, "pre", None) is not None:
                tbase = ws.pre
                ws.pre = None
            else:
                tbase = len(ws.tiles)
                ws.tiles.extend(wtiles_l[0])
            cnt = [0]

            def bpop():
                cnt[0] += 1
                if cnt[0] % 5 == 0:
                    g.pop_tasks(16 if cnt[0] <= 40 else 10)
                else:
                    g.pop_tasks(0)
            ffn_head(c)
            ffn_pass(c, 0, ws, tbase, bpop)
            ffn_pass(c, 1, ws, tbase + NFC + 2 * NDC, bpop)

        def mamba_prologue(c):
            xn = xnb[c % 2]
            bxn = bXN[c % 2]
            x = xb[c % 2]
            bx = bX[c % 2]
            for k in (range(NDC) if DBG_PHASE != "dl" else []):
                g.op("V", (lambda k_: lambda e: e.scalar_tensor_tensor(
                    x[:, k_ * TC:(k_ + 1) * TC], xn[:, k_ * TC:(k_ + 1) * TC],
                    dvec[:, k_:k_ + 1], x[:, k_ * TC:(k_ + 1) * TC],
                    ALU.mult, ALU.add))(k),
                    reads=[bxn, bConst], writes=[bx[k]])
            def drfn(k):
                return lambda e: e.matmul(
                    ps[0][:, :], xpw[:, k * 160:k * 160 + 128],
                    xn[:, k * TC:(k + 1) * TC], start=(k == 0), stop=(k == NDC - 1))
            def bcfn(k):
                return lambda e: e.matmul(
                    ps[1][0:32, :], xpw[:, k * 160 + 128:k * 160 + 160],
                    xn[:, k * TC:(k + 1) * TC], start=(k == 0), stop=(k == NDC - 1))
            mms = []
            for k in range(NDC):
                mms.append(drfn(k))
                mms.append(bcfn(k))
            g.op("T", mms, reads=[bxn, bConst], writes=[bP[0], bP[1]])
            g.op("S", lambda e: e.copy(drs[:, :], ps[0][:, :]),
                 reads=[bP[0]], writes=[bDrs])
            g.op("S", lambda e: e.copy(bcs[:, :], ps[1][0:32, :]),
                 reads=[bP[1]], writes=[bBcs])
            for q in range(NQ):
                for j in range(GN):
                    g.push_unit(make_burst(c, q, j))
                for dc in range(NDC):
                    g.push_unit(make_delta(c, q, dc))
                    for j in range(GN):
                        g.push_unit(make_unit(c, q, dc, j))

        def make_burst(c, q, j):
            def burst():
                n = q * GN + j
                for row, dstbuf, dst in ((n, bBbh, Bbh), (16 + n, bCbh, Cbh)):
                    g.op("T", (lambda row_: lambda e: e.matmul(
                        ps[6][0:1, :], selc[0:32, row_:row_ + 1], bcs[:, :],
                        start=True, stop=True))(row),
                        reads=[bBcs, bConst], writes=[bP[6]])
                    g.op("S", lambda e: e.copy(stg[:, :], ps[6][0:1, :]),
                         reads=[bP[6]], writes=[bO1])
                    g.op("T", lambda e: e.matmul(ps[7][:, :], on1b[:, :], stg[:, :],
                                                 start=True, stop=True),
                         reads=[bO1, bConst], writes=[bP[7]])
                    g.op("S", (lambda dst_, j_: lambda e: e.copy(
                        dst_[:, j_ * TC:(j_ + 1) * TC], ps[7][:, :]))(dst, j),
                        reads=[bP[7]], writes=[dstbuf])
            return burst

        def make_delta(c, q, dc):
            def delta():
                bank = 6 if dc % 2 == 0 else 7
                g.op("T", (lambda bank_, dc_: lambda e: e.matmul(
                    ps[bank_][:, :], dtw[:, dc_ * 128:(dc_ + 1) * 128], drs[:, :],
                    start=True, stop=True))(bank, dc),
                    reads=[bDrs, bConst], writes=[bP[bank]])
                g.op("S", (lambda bank_, dc_: lambda e: e.activation(
                    out=sp_a[:, :], in_=ps[bank_][:, :], func=AF.Relu,
                    bias=dtb[:, dc_:dc_ + 1]))(bank, dc),
                    reads=[bP[bank], bConst], writes=[bSpA, bS0])
                g.op("S", (lambda bank_, dc_: lambda e: e.activation(
                    out=sp_b[:, :], in_=ps[bank_][:, :], func=AF.Abs,
                    bias=dtb[:, dc_:dc_ + 1]))(bank, dc),
                    reads=[bP[bank], bConst], writes=[bSpB, bS1])
                g.op("S", lambda e: e.activation(out=delta_t[:, :], in_=sp_b[:, :],
                                                 func=AF.Exp, scale=-1.0),
                     reads=[bSpB], writes=[bDelta])
                g.op("S", lambda e: e.activation(out=sp_b[:, :], in_=delta_t[:, :],
                                                 func=AF.Ln, bias=1.0),
                     reads=[bDelta, bConst], writes=[bSpB, bS1])
                g.op("V", lambda e: e.tensor_tensor(delta_t[:, :], sp_a[:, :],
                                                    sp_b[:, :], ALU.add),
                     reads=[bSpA, bSpB], writes=[bDelta])
                xn = xnb[c % 2]
                g.op("V", (lambda dc_: lambda e: e.tensor_tensor(
                    du_t[:, :], delta_t[:, :], xn[:, dc_ * TC:(dc_ + 1) * TC],
                    ALU.mult))(dc),
                    reads=[bDelta, bXN[c % 2]], writes=[bDu])
                if DBG_PHASE == "dl" and q == 0:
                    x = xb[c % 2]
                    g.op("V", (lambda dc_: lambda e: e.tensor_copy(
                        x[:, dc_ * TC:(dc_ + 1) * TC], delta_t[:, :]))(dc),
                        reads=[bDelta], writes=[bX[c % 2][dc]])
            return delta

        def make_unit(c, q, dc, j):
            def unit():
                U = unit_ctr[0]
                unit_ctr[0] += 1
                n = q * GN + j
                idx = dc * NST + n
                r = U % 2
                ra = U % RDA
                g.op("S", (lambda ra_, idx_: lambda e: e.activation(
                    out=dA_r[ra_][:, :], in_=delta_t[:, :], func=AF.Exp,
                    scale=negA[:, idx_:idx_ + 1]))(ra, idx),
                    reads=[bDelta, bPosA], writes=[bDA[ra]])
                g.op("V", (lambda r_, j_: lambda e: e.tensor_tensor(
                    dBu_r[r_][:, :], du_t[:, :], Bbh[:, j_ * TC:(j_ + 1) * TC],
                    ALU.mult))(r, j),
                    reads=[bDu, bBbh], writes=[bDBu[r]])
                carry = savepos.get(U - NDC * NST)
                if carry:
                    g.p.wait("V", "V", carry)
                else:
                    for s_, v_ in bHst.wr.d.items():
                        g.p.wait("V", s_, v_)
                g.op("V", (lambda r_, ra_, idx_: lambda e: e.tensor_tensor_scan(
                    h_r[r_][:, :], dA_r[ra_][:, :], dBu_r[r_][:, :],
                    h_state[:, idx_:idx_ + 1], ALU.mult, ALU.add))(r, ra, idx),
                    reads=[bDA[ra], bDBu[r]], writes=[bHr[r]])
                p = g.op("V", (lambda r_, idx_: lambda e: e.tensor_copy(
                    h_state[:, idx_:idx_ + 1], h_r[r_][:, TC - 1:TC]))(r, idx),
                    reads=[bHr[r]])
                savepos[U] = p
                if DBG_PHASE == "dl":
                    return
                x = xb[c % 2]
                dst = 0 if j == 0 else 1
                g.op("G", (lambda r_, j_, dst_: lambda e: e.tensor_tensor(
                    ymt[dst_][:, :], h_r[r_][:, :], Cbh[:, j_ * TC:(j_ + 1) * TC],
                    ALU.mult))(r, j, dst),
                    reads=[bHr[r], bCbh], writes=[bYmt[dst]])
                if j > 0:
                    g.op("G", lambda e: e.tensor_tensor(
                        ymt[0][:, :], ymt[0][:, :], ymt[1][:, :], ALU.add),
                        reads=[bYmt[1]], writes=[bYmt[0]])
                if j == GN - 1:
                    g.op("G", (lambda dc_: lambda e: e.tensor_tensor(
                        x[:, dc_ * TC:(dc_ + 1) * TC], x[:, dc_ * TC:(dc_ + 1) * TC],
                        ymt[0][:, :], ALU.add))(dc),
                        reads=[bYmt[0]], writes=[bX[c % 2][dc]])
            return unit

        load_x(0)
        load_x(1)
        wtiles = []
        for e_i in range(2):
            for mc in range(32):
                wtiles.append(wah[e_i][mc])
            for mq in range(16):
                wtiles.append(wbh[e_i][mq, 0])
                wtiles.append(wbh[e_i][mq, 1])
        wtiles_l = [wtiles]
        gws = WStream()
        if DBG_PHASE is not None:
            for c in range(NCH):
                layernorm(c)
                if DBG_PHASE == "ln":
                    xd, xnd = xb[c % 2], xnb[c % 2]
                    for k in range(NDC):
                        g.op("V", (lambda k_, xd_, xnd_: lambda e: e.tensor_copy(
                            xd_[:, k_ * TC:(k_ + 1) * TC],
                            xnd_[:, k_ * TC:(k_ + 1) * TC]))(k, xd, xnd),
                            reads=[bXN[c % 2]], writes=[bX[c % 2][k]])
                if DBG_PHASE == "ffa":
                    ws = WStream(); ws.tiles = wtiles
                    ffn_pass(c, 0, ws, 0, lambda: g.pop_tasks(2), only_a=True)
                    xd = xb[c % 2]
                    for k in range(NDC):
                        g.op("V", (lambda k_, xd_: lambda e: e.tensor_copy(
                            xd_[:, k_ * TC:(k_ + 1) * TC],
                            hbuf[:, k_ * TC:(k_ + 1) * TC]))(k, xd),
                            reads=[bH], writes=[bX[c % 2][k]])
                if DBG_PHASE == "ffb":
                    ws = WStream(); ws.tiles = wtiles
                    ffn_pass(c, 0, ws, 0, lambda: g.pop_tasks(2))
                    xd = xb[c % 2]
                    for k in range(NDC):
                        g.op("V", (lambda k_, xd_: lambda e: e.tensor_copy(
                            xd_[:, k_ * TC:(k_ + 1) * TC],
                            out0[:, k_ * TC:(k_ + 1) * TC]))(k, xd),
                            reads=[bO0], writes=[bX[c % 2][k]])
                if DBG_PHASE not in ("id", "ln", "ffa", "ffb"):
                    ws = WStream()
                    ffn_phase(c, ws)
                    g.flush()
                if DBG_PHASE in ("mamba", "ffn2", "dl"):
                    layernorm(c)
                    mamba_prologue(c)
                    g.flush()
                if DBG_PHASE == "ffn2":
                    layernorm(c)
                    ws = WStream()
                    ffn_phase(c, ws)
                    g.flush()
                store_x(c)
                if c + 2 < NCH:
                    load_x(c + 2)
            g.p.ops["Y"].append(("w", "O", g.p.ct["O"]))
        for it in (range(NCH + 1) if DBG_PHASE is None else []):
            if it < NCH:
                c = it
                layernorm(c)
                ffn_phase(c, gws)
                g.flush()
            if it >= 1:
                g.flush()
                layernorm(it - 1)
            if it < NCH:
                layernorm(c)
                gws.tiles.extend(wtiles)
                gws.pre = len(gws.tiles) - len(wtiles)
                gws.ensure(gws.pre + 3)
                mamba_prologue(c)
            if it >= 1:
                cm = it - 1
                ffn_phase(cm, gws)
                store_x(cm)
                if cm + 2 < NCH:
                    load_x(cm + 2)
        g.p.ops["Y"].append(("w", "O", g.p.ct["O"]))

        def replay(name):
            def run(eng):
                for item in g.p.ops[name]:
                    if item[0] == "w":
                        eng.wait_ge(sems[item[1]], item[2])
                    else:
                        inst = item[1](eng)
                        if item[2] is not None:
                            inst.then_inc(sems[item[2]], item[3])
            return run

        @block.tensor
        def _(tensor):
            replay("T")(tensor)

        @block.scalar
        def _(scalar):
            replay("S")(scalar)

        @block.vector
        def _(vector):
            replay("V")(vector)

        @block.gpsimd
        def _(gpsimd):
            replay("G")(gpsimd)

        @block.sync
        def _(sync):
            replay("Y")(sync)

    return nc


def _prep_weights(i):
    out = {}
    for e, (wa_k, ba_k, wb_k, bb_k) in enumerate(
            [("L0_ffn_W0a", "L0_ffn_b0a", "L0_ffn_W0b", "L0_ffn_b0b"),
             ("L0_ffn_W1a", "L0_ffn_b1a", "L0_ffn_W1b", "L0_ffn_b1b")]):
        wa = np.asarray(i[wa_k], dtype=np.float32)
        wb = np.asarray(i[wb_k], dtype=np.float32)
        bb = np.asarray(i[bb_k], dtype=np.float32)
        wah = np.zeros((32, 128, WCOL), dtype=BF)
        wah[:, :, :2048] = wa.reshape(32, 128, 16, 128).transpose(
            0, 3, 2, 1).reshape(32, 128, 2048).astype(BF)
        wbh = np.zeros((16, 2, 128, WCOL), dtype=BF)
        wbt = wb.reshape(16, 128, 32, 128).transpose(0, 2, 3, 1)
        wbh[:, 0, :, :2048] = wbt[:, 0:16].transpose(0, 2, 1, 3).reshape(
            16, 128, 2048).astype(BF)
        wbh[:, 1, :, :2048] = wbt[:, 16:32].transpose(0, 2, 1, 3).reshape(
            16, 128, 2048).astype(BF)
        wbh[:, 1, 0, 2048:2176] = bb.reshape(16, 128).astype(BF)
        out[f"wa{e}"] = wah
        out[f"wb{e}"] = wbh
        out[f"b{e}a_h"] = np.ascontiguousarray(
            np.asarray(i[ba_k], dtype=np.float32).reshape(32, 128).T)
    xpw = np.asarray(i["L0_attn_x_proj_w"], dtype=np.float32)
    out["xpw_h"] = np.ascontiguousarray(
        xpw.reshape(160, 16, 128).transpose(2, 1, 0)).reshape(128, 16 * 160).astype(BF)
    dtw = np.asarray(i["L0_attn_dt_proj_w"], dtype=np.float32)
    out["dtw_h"] = np.ascontiguousarray(
        dtw.reshape(16, 128, 128).transpose(2, 0, 1)).reshape(128, 2048).astype(BF)
    wgm = np.asarray(i["L0_ffn_Wg"], dtype=np.float32)
    out["wgf_h"] = np.ascontiguousarray(
        wgm.reshape(2, 16, 128).transpose(2, 1, 0).reshape(128, 32))
    selc = np.zeros((32, 32), dtype=BF)
    for r in range(32):
        selc[r, r] = 1.0
    out["selc_h"] = selc
    out["alog_h"] = np.ascontiguousarray(
        np.asarray(i["L0_attn_A_log"], dtype=np.float32).reshape(
            16, 128, 16).transpose(1, 0, 2)).reshape(128, 256)
    out["dvec_h"] = np.ascontiguousarray(
        np.asarray(i["L0_attn_D"], dtype=np.float32).reshape(16, 128).T)
    out["dtb_h"] = np.ascontiguousarray(
        np.asarray(i["L0_attn_dt_proj_b"], dtype=np.float32).reshape(16, 128).T)
    out["onesb_h"] = np.ones((1, TC), dtype=BF)
    out["on1f_h"] = np.ones((1, 128), dtype=np.float32)
    out["onesk_h"] = np.ones((128, 1), dtype=np.float32)
    return out


def kernel(**inputs):
    shared = _prep_weights(inputs)
    x = np.asarray(inputs["L0_attn_x"], dtype=np.float32)
    in_maps = []
    for b in range(8):
        m = dict(shared)
        m["xT"] = np.ascontiguousarray(x[b].T)
        in_maps.append(m)
    nc = build_program()
    res = run_bass_kernel_spmd(nc, in_maps, core_ids=list(range(8)))
    outs = [np.ascontiguousarray(np.asarray(r["yT"], dtype=np.float32).T)
            for r in res.results]
    return np.stack(outs, axis=0)


# version 26
# speedup vs baseline: 1.0551x; 1.0549x over previous
import contextlib
import numpy as np
import ml_dtypes
import concourse.bass as bass
import concourse.mybir as mybir
from concourse.bass_utils import run_bass_kernel_spmd

F32 = mybir.dt.float32
BF16 = mybir.dt.bfloat16
U8 = mybir.dt.uint8
AF = mybir.ActivationFunctionType
ALU = mybir.AluOpType
BF = ml_dtypes.bfloat16

DBG_PHASE = None

DM = 2048
SEQ = 4096
TC = 512
NCH = SEQ // TC
NDC = DM // 128
NFC = 4096 // 128
NST = 16
NQ = 4
GN = NST // NQ
WCOL = 2176
EPS = 1e-5


class Dep:
    __slots__ = ("d",)

    def __init__(self):
        self.d = {}

    def add(self, sem, val):
        if val and self.d.get(sem, 0) < val:
            self.d[sem] = val


class Buf:
    __slots__ = ("rd", "wr")

    def __init__(self):
        self.rd = Dep()
        self.wr = Dep()


class Prog:
    def __init__(self):
        self.ops = {e: [] for e in "TSVGY"}
        self.ct = {"T": 0, "S": 0, "V": 0, "G": 0, "D": 0, "O": 0, "C": 0}
        self.waited = {e: {} for e in "TSVGY"}

    def wait(self, eng, sem, val):
        if val <= 0:
            return
        w = self.waited[eng]
        if w.get(sem, 0) >= val:
            return
        w[sem] = val
        self.ops[eng].append(("w", sem, val))

    def raw_emit(self, eng, fn, inc, amt):
        pos = None
        if inc is not None:
            self.ct[inc] += amt
            pos = self.ct[inc]
        self.ops[eng].append(("i", fn, inc, amt))
        return pos


ESEM = {"T": "T", "S": "S", "V": "V", "G": "G"}


class Gen:

    def __init__(self):
        self.p = Prog()
        self.pending_units = []
        self.pending_prio = []

    def op(self, eng, fns, reads=(), writes=(), inc=True, amt=1, sem=None):
        p = self.p
        dep = Dep()
        for b in reads:
            for s, v in b.wr.d.items():
                dep.add(s, v)
        for b in writes:
            for s, v in b.rd.d.items():
                dep.add(s, v)
            for s, v in b.wr.d.items():
                dep.add(s, v)
        for s, v in sorted(dep.d.items()):
            p.wait(eng, s, v)
        if not isinstance(fns, (list, tuple)):
            fns = [fns]
        pos = None
        if eng == "Y":
            for fn in fns[:-1]:
                p.raw_emit(eng, fn, None, 0)
            pos = p.raw_emit(eng, fns[-1], sem or "D", 16)
            s_used = sem or "D"
        else:
            s_used = ESEM[eng]
            for fn in fns[:-1]:
                p.raw_emit(eng, fn, None, 0)
            if inc:
                pos = p.raw_emit(eng, fns[-1], s_used, amt)
            else:
                p.raw_emit(eng, fns[-1], None, 0)
        for b in reads:
            if pos:
                b.rd.add(s_used, pos)
        for b in writes:
            if pos:
                b.wr = Dep()
                b.wr.add(s_used, pos)
                b.rd = Dep()
        return pos

    def push_unit(self, fn):
        self.pending_units.append(fn)

    def push_prio(self, fn):
        self.pending_prio.append(fn)

    def pop_tasks(self, k):
        n = 0
        while self.pending_prio and n < k:
            self.pending_prio.pop(0)()
            n += 1
        while self.pending_units and n < k:
            self.pending_units.pop(0)()
            n += 1

    def flush(self):
        while self.pending_prio:
            self.pending_prio.pop(0)()
        while self.pending_units:
            self.pending_units.pop(0)()


def build_program():
    nc = bass.Bass("TRN2", target_bir_lowering=False)

    xT = nc.dram_tensor("xT", [DM, SEQ], F32, kind="ExternalInput").ap()
    yT = nc.dram_tensor("yT", [DM, SEQ], F32, kind="ExternalOutput").ap()
    wah = [nc.dram_tensor(f"wa{e}", [32, 128, WCOL], BF16, kind="ExternalInput").ap()
           for e in range(2)]
    wbh = [nc.dram_tensor(f"wb{e}", [16, 2, 128, WCOL], BF16, kind="ExternalInput").ap()
           for e in range(2)]
    xpw_h = nc.dram_tensor("xpw_h", [128, NDC * 160], BF16, kind="ExternalInput").ap()
    dtw_h = nc.dram_tensor("dtw_h", [128, DM], BF16, kind="ExternalInput").ap()
    wgf_h = nc.dram_tensor("wgf_h", [128, NDC * 2], F32, kind="ExternalInput").ap()
    selc_h = nc.dram_tensor("selc_h", [32, 32], BF16, kind="ExternalInput").ap()
    alog_h = nc.dram_tensor("alog_h", [128, NDC * NST], F32, kind="ExternalInput").ap()
    dvec_h = nc.dram_tensor("dvec_h", [128, NDC], F32, kind="ExternalInput").ap()
    dtb_h = nc.dram_tensor("dtb_h", [128, NDC], F32, kind="ExternalInput").ap()
    b0a_h = nc.dram_tensor("b0a_h", [128, NFC], F32, kind="ExternalInput").ap()
    b1a_h = nc.dram_tensor("b1a_h", [128, NFC], F32, kind="ExternalInput").ap()
    onesb_h = nc.dram_tensor("onesb_h", [1, TC], BF16, kind="ExternalInput").ap()
    on1f_h = nc.dram_tensor("on1f_h", [1, 128], F32, kind="ExternalInput").ap()
    onesk_h = nc.dram_tensor("onesk_h", [128, 1], F32, kind="ExternalInput").ap()

    with contextlib.ExitStack() as st:
        def sb(name, shape, dt=F32):
            return st.enter_context(nc.sbuf_tensor(name, shape, dt))

        xb = [sb("xb0", [128, NDC * TC]), sb("xb1", [128, NDC * TC])]
        xnb = [sb("xnb0", [128, NDC * TC], BF16), sb("xnb1", [128, NDC * TC], BF16)]
        hbuf = sb("hbuf", [128, NFC * TC], BF16)
        out0 = sb("out0", [128, NDC * TC], BF16)
        out1 = sb("out1", [128, TC], BF16)
        wr = [sb(f"wr{i}", [128, WCOL], BF16) for i in range(3)]
        xpw = sb("xpw", [128, NDC * 160], BF16)
        dtw = sb("dtw", [128, DM], BF16)
        wgf = sb("wgf", [128, NDC * 2])
        sdif = sb("sdif", [1, 1])
        selc = sb("selc", [32, 32], BF16)
        negA = sb("negA", [128, NDC * NST])
        dvec = sb("dvec", [128, NDC])
        dtb = sb("dtb", [128, NDC])
        b0a = sb("b0a", [128, NFC])
        b1a = sb("b1a", [128, NFC])
        onesb = sb("onesb", [1, TC], BF16)
        on1b = onesb[0:1, 0:128]
        on1f = sb("on1f", [1, 128])
        onesk = sb("onesk", [128, 1])
        drs = sb("drs", [128, TC], BF16)
        bcs = sb("bcs", [32, TC], BF16)
        Bbh = sb("Bbh", [128, GN * TC], BF16)
        Cbh = sb("Cbh", [128, GN * TC], BF16)
        stg = out1[0:1, :]
        delta_t = sb("delta_t", [128, TC])
        du_t = sb("du_t", [128, TC], BF16)
        RDA = 3
        dA_r = [sb(f"dA{i}", [128, TC]) for i in range(RDA)]
        dBu_r = [sb("dBu0", [128, TC], BF16), sb("dBu1", [128, TC], BF16)]
        h_r = [sb("h0", [128, TC], BF16), sb("h1", [128, TC], BF16)]
        ymt = [sb("ymt0", [128, TC], BF16), sb("ymt1", [128, TC], BF16)]
        h_state = sb("h_state", [128, NDC * NST])
        sp_a = sb("sp_a", [128, TC])
        sp_b = sb("sp_b", [128, TC])
        sq = [sb("sq0", [128, TC], BF16), sb("sq1", [128, TC], BF16)]
        st2 = sb("st2", [1, TC])
        onesk_bf = sb("onesk_bf", [128, 1], BF16)
        mask_i = sb("mask_i", [128, TC], U8)

        ps = [st.enter_context(nc.psum_tensor(f"p{i}", [128, TC], F32))
              for i in range(8)]

        if globals().get("PROBE_SBUF"):
            lo, hi = 0, 300000
            while lo < hi - 64:
                mid = (lo + hi) // 2
                try:
                    cm = nc.sbuf_tensor("probe_x", [128, mid], U8)
                    cm.__enter__()
                    cm.__exit__(None, None, None)
                    lo = mid
                except Exception:
                    hi = mid
            print(f"SBUF headroom: {lo} bytes/partition")

        sems = {}
        for s in ("T", "S", "V", "G", "D", "O", "C"):
            sems[s] = st.enter_context(nc.semaphore(f"sem{s}"))
        block = st.enter_context(nc.Block())

        s0 = sp_a[0:1, :]
        s1 = sp_b[0:1, :]
        s2v = st2[0:1, :]

        g = Gen()

        bX = [[Buf() for _ in range(NDC)], [Buf() for _ in range(NDC)]]
        bXN = [Buf(), Buf()]
        bH = Buf()
        bO0 = Buf()
        bO1 = Buf()
        bWr = [Buf(), Buf(), Buf()]
        bP = [Buf() for _ in range(8)]
        bDrs, bBcs = Buf(), Buf()
        bBbh, bCbh = Buf(), Buf()
        bDelta, bDu = Buf(), Buf()
        bDA = [Buf() for _ in range(RDA)]
        bDBu = [Buf(), Buf()]
        bHr = [Buf(), Buf()]
        bYmt = [Buf(), Buf()]
        bSpA = Buf()
        bSpB = Buf()
        bSq = [Buf(), Buf()]
        bS0, bS1, bS2 = Buf(), Buf(), Buf()
        bMask = Buf()
        bConst = Buf()
        bPosA, bNdtb = Buf(), Buf()
        bHst = Buf()

        savepos = {}
        unit_ctr = [0]
        gctr = [0]

        def dma(dst_ap, src_ap):
            return lambda e: e.dma_start(out=dst_ap, in_=src_ap)

        const_loads = [
            (xpw[:, :], xpw_h), (dtw[:, :], dtw_h), (wgf[:, :], wgf_h),
            (selc[:, :], selc_h), (negA[:, :], alog_h), (dvec[:, :], dvec_h),
            (dtb[:, :], dtb_h), (b0a[:, :], b0a_h), (b1a[:, :], b1a_h),
            (onesb[:, :], onesb_h), (on1f[:, :], on1f_h),
            (onesk[:, :], onesk_h),
        ]
        for d, s in const_loads:
            g.op("Y", dma(d, s), writes=[bConst], sem="C")
        g.op("S", lambda e: e.activation(out=sp_a[:, 0:NDC * NST], in_=negA[:, :],
                                         func=AF.Exp),
             reads=[bConst], writes=[bSpA])
        g.op("S", lambda e: e.mul(negA[:, :], sp_a[:, 0:NDC * NST], -1.0),
             reads=[bSpA], writes=[bPosA])
        g.op("V", lambda e: e.memset(h_state[:, :], 0.0), writes=[bHst])
        g.op("V", lambda e: e.tensor_copy(onesk_bf[:, :], onesk[:, :]),
             reads=[bConst], writes=[bConst])
        nc.const_aps.register(F32, 1.0, onesk[:, :])
        bSdif = Buf()
        for e_i, bank in ((0, 2), (1, 3)):
            def sfn(k, e_i_, bank_):
                return lambda e: e.matmul(
                    ps[bank_][0:1, 0:1], wgf[:, k * 2 + e_i_:k * 2 + e_i_ + 1],
                    onesk[:, :], start=(k == 0), stop=(k == NDC - 1))
            g.op("T", [sfn(k, e_i, bank) for k in range(NDC)],
                 reads=[bConst], writes=[bP[bank]])
        g.op("S", lambda e: e.copy(st2[0:1, 0:1], ps[2][0:1, 0:1]),
             reads=[bP[2]], writes=[bS2])
        g.op("V", lambda e: e.tensor_tensor(sdif[:, :], st2[0:1, 0:1],
                                            ps[3][0:1, 0:1], ALU.subtract),
             reads=[bS2, bP[3]], writes=[bSdif])

        xT3 = xT.rearrange("(dc p) t -> p dc t", p=128)
        yT3 = yT.rearrange("(dc p) t -> p dc t", p=128)

        def load_x(c):
            buf = xb[c % 2]
            b3 = buf.rearrange("p (dc t) -> p dc t", dc=NDC)
            for gq in range(4):
                g.op("Y", dma(b3[:, gq * 4:(gq + 1) * 4, :],
                              xT3[:, gq * 4:(gq + 1) * 4, c * TC:(c + 1) * TC]),
                     writes=bX[c % 2][gq * 4:(gq + 1) * 4])

        def store_x(c):
            while g.pending_prio:
                g.pending_prio.pop(0)()
            buf = xb[c % 2]
            b3 = buf.rearrange("p (dc t) -> p dc t", dc=NDC)
            for gq in range(4):
                g.op("Y", dma(yT3[:, gq * 4:(gq + 1) * 4, c * TC:(c + 1) * TC],
                              b3[:, gq * 4:(gq + 1) * 4, :]),
                     reads=bX[c % 2][gq * 4:(gq + 1) * 4], sem="O")

        def layernorm(c):
            x = xb[c % 2]
            xn = xnb[c % 2]
            bx, bxn = bX[c % 2], bXN[c % 2]
            def s1fn(k):
                return lambda e: e.matmul(ps[2][0:1, :], onesk[:, :],
                                          x[:, k * TC:(k + 1) * TC],
                                          start=(k == 0), stop=(k == NDC - 1))
            for k in range(NDC):
                g.op("T", s1fn(k), reads=[bx[k], bConst], writes=[bP[2]])
            for k in range(NDC):
                sl = sq[k % 2]
                g.op("S", (lambda sl_, k_: lambda e: e.activation(
                    out=sl_[:, :], in_=x[:, k_ * TC:(k_ + 1) * TC], func=AF.Square))(sl, k),
                    reads=[bx[k]], writes=[bSq[k % 2]])
                g.op("T", (lambda sl_, k_: lambda e: e.matmul(
                    ps[3][0:1, :], onesk_bf[:, :], sl_[:, :],
                    start=(k_ == 0), stop=(k_ == NDC - 1)))(sl, k),
                    reads=[bSq[k % 2], bConst], writes=[bP[3]])
            g.op("V", lambda e: e.tensor_scalar(s0, ps[2][0:1, :], 1.0 / DM,
                                                None, ALU.mult),
                 reads=[bP[2]], writes=[bS0, bSpA])
            g.op("V", lambda e: e.tensor_scalar(s1, ps[3][0:1, :], 1.0 / DM,
                                                EPS, ALU.mult, ALU.add),
                 reads=[bP[3]], writes=[bS1, bSpB])
            g.op("V", lambda e: e.tensor_tensor(s2v, s0, s0, ALU.mult),
                 reads=[bS0], writes=[bS2])
            g.op("V", lambda e: e.tensor_tensor(s1, s1, s2v, ALU.subtract),
                 reads=[bS2], writes=[bS1, bSpB])
            g.op("S", lambda e: e.activation(out=s2v, in_=s1, func=AF.Ln),
                 reads=[bS1], writes=[bS2])
            g.op("S", lambda e: e.activation(out=s1, in_=s2v, func=AF.Exp,
                                             scale=-0.5),
                 reads=[bS2], writes=[bS1, bSpB])
            g.op("T", lambda e: e.matmul(ps[5][:, :], on1f[:, :], s1,
                                         start=True, stop=True),
                 reads=[bS1, bConst], writes=[bP[5]])
            g.op("V", lambda e: e.tensor_tensor(s2v, s0, s1, ALU.mult),
                 reads=[bS0, bS1], writes=[bS2])
            g.op("T", lambda e: e.matmul(ps[4][:, :], on1f[:, :], s2v,
                                         start=True, stop=True),
                 reads=[bS2, bConst], writes=[bP[4]])
            for k in range(NDC):
                xv = x[:, k * TC:(k + 1) * TC]
                nv = xn[:, k * TC:(k + 1) * TC]
                g.op("V", (lambda xv_, nv_: lambda e: e.tensor_tensor(
                    nv_, xv_, ps[5][:, :], ALU.mult))(xv, nv),
                    reads=[bx[k], bP[5]], writes=[] if k else [bxn])
            for k in range(NDC):
                nv = xn[:, k * TC:(k + 1) * TC]
                g.op("V", (lambda nv_: lambda e: e.tensor_tensor(
                    nv_, nv_, ps[4][:, :], ALU.subtract))(nv),
                    reads=[bP[4]], writes=[bxn] if k == NDC - 1 else [])

        def ffn_head(c):
            x = xb[c % 2]
            bx = bX[c % 2]
            for e_i, bank in ((0, 2), (1, 3)):
                def lfn(k, e_i_, bank_):
                    return lambda e: e.matmul(
                        ps[bank_][0:1, :], wgf[:, k * 2 + e_i_:k * 2 + e_i_ + 1],
                        x[:, k * TC:(k + 1) * TC],
                        start=(k == 0), stop=(k == NDC - 1))
                g.op("T", [lfn(k, e_i, bank) for k in range(NDC)],
                     reads=list(bx) + [bConst], writes=[bP[bank]])
            g.op("S", lambda e: e.copy(s2v, ps[2][0:1, :]),
                 reads=[bP[2]], writes=[bS2])
            g.op("V", lambda e: e.tensor_tensor(s2v, s2v, ps[3][0:1, :],
                                                ALU.subtract),
                 reads=[bP[3]], writes=[bS2])
            g.op("V", lambda e: e.tensor_scalar(s1, s0, sdif[0:1, 0:1], None,
                                                ALU.mult),
                 reads=[bS0, bSdif], writes=[bS1, bSpB])
            g.op("V", lambda e: e.tensor_tensor(s2v, s2v, s1, ALU.subtract),
                 reads=[bS1], writes=[bS2])
            g.op("T", lambda e: e.matmul(ps[4][:, :], on1f[:, :], s2v,
                                         start=True, stop=True),
                 reads=[bS2, bConst], writes=[bP[4]])
            g.op("V", lambda e: e.tensor_single_scalar(mask_i[:, :], ps[4][:, :],
                                                       0.0, ALU.is_ge),
                 reads=[bP[4]], writes=[bMask])

        class WStream:
            def __init__(self):
                self.tiles = []
                self.cursor = 0

            def ensure(self, upto):
                while self.cursor < min(upto, len(self.tiles)):
                    i = self.cursor
                    s = i % 3
                    g.op("Y", dma(wr[s][:, :], self.tiles[i]), writes=[bWr[s]])
                    self.cursor += 1

        def ffn_pass(c, e_i, ws, tbase, pop, only_a=False):
            xn = xnb[c % 2]
            bxn = bXN[c % 2]
            bx = bX[c % 2]
            x = xb[c % 2]
            act = AF.Gelu if e_i == 0 else AF.Relu
            brow = (b0a if e_i == 0 else b1a)
            for mc in range(NFC):
                t_i = tbase + mc
                ws.ensure(t_i + 3)
                s_i = t_i % 3
                bank = gctr[0] % 4
                gctr[0] += 1
                def afn(k, s_i_, bank_):
                    return lambda e: e.matmul(
                        ps[bank_][:, :], wr[s_i_][:, k * 128:(k + 1) * 128],
                        xn[:, k * TC:(k + 1) * TC],
                        start=(k == 0), stop=(k == NDC - 1))
                g.op("T", [afn(k, s_i, bank) for k in range(NDC)],
                     reads=[bxn, bWr[s_i]], writes=[bP[bank]])
                g.op("S", (lambda mc_, bank_: lambda e: e.activation(
                    out=hbuf[:, mc_ * TC:(mc_ + 1) * TC], in_=ps[bank_][:, :],
                    func=act, bias=brow[:, mc_:mc_ + 1]))(mc, bank),
                    reads=[bP[bank], bConst],
                    writes=[bH] if mc in (0, NFC - 1) else [])
                pop()
            if only_a:
                return
            for mq in range(NDC):
                bank = gctr[0] % 4
                gctr[0] += 1
                t0 = tbase + NFC + 2 * mq
                ws.ensure(t0 + 3)
                sA, sB = t0 % 3, (t0 + 1) % 3
                def bfn(k, s_, bank_):
                    return lambda e: e.matmul(
                        ps[bank_][:, :], wr[s_][:, (k % 16) * 128:(k % 16 + 1) * 128],
                        hbuf[:, k * TC:(k + 1) * TC],
                        start=(k == 0), stop=False)
                g.op("T", [bfn(k, sA, bank) for k in range(16)],
                     reads=[bH, bWr[sA]], writes=[bP[bank]])
                def bias_mm(s_, bank_):
                    return lambda e: e.matmul(
                        ps[bank_][:, :], wr[s_][0:1, 2048:2176], onesb[:, :],
                        start=False, stop=True)
                g.op("T", [bfn(k, sB, bank) for k in range(16, 32)] +
                     [bias_mm(sB, bank)],
                     reads=[bH, bWr[sB], bConst], writes=[bP[bank]])
                if e_i == 0:
                    g.op("S", (lambda mq_, bank_: lambda e: e.copy(
                        out0[:, mq_ * TC:(mq_ + 1) * TC], ps[bank_][:, :]))(mq, bank),
                        reads=[bP[bank]],
                        writes=[bO0] if mq in (0, NDC - 1) else [])
                else:
                    while g.pending_prio:
                        g.pending_prio.pop(0)()
                    g.op("S", (lambda bank_: lambda e: e.copy(
                        out1[:, :], ps[bank_][:, :]))(bank),
                        reads=[bP[bank]], writes=[bO1])

                    def combine(mq_=mq, c_=c):
                        g.op("V", lambda e: e.copy_predicated(
                            out1[:, :], mask_i[:, :],
                            out0[:, mq_ * TC:(mq_ + 1) * TC]),
                            reads=[bMask, bO0], writes=[bO1])
                        g.op("V", lambda e: e.scalar_tensor_tensor(
                            x[:, mq_ * TC:(mq_ + 1) * TC], out1[:, :], 0.5,
                            x[:, mq_ * TC:(mq_ + 1) * TC], ALU.mult, ALU.add),
                            reads=[bO1], writes=[bx[mq_]])
                    g.push_prio(combine)
                pop()

        def ffn_phase(c, ws, tbase=None, pop=None):
            if getattr(ws, "pre", None) is not None:
                tbase = ws.pre
                ws.pre = None
            else:
                tbase = len(ws.tiles)
                ws.tiles.extend(wtiles_l[0])
            cnt = [0]

            def bpop():
                cnt[0] += 1
                if cnt[0] % 5 == 0:
                    g.pop_tasks(12)
                else:
                    g.pop_tasks(0)
            ffn_head(c)
            ffn_pass(c, 0, ws, tbase, bpop)
            ffn_pass(c, 1, ws, tbase + NFC + 2 * NDC, bpop)

        def mamba_prologue(c):
            xn = xnb[c % 2]
            bxn = bXN[c % 2]
            x = xb[c % 2]
            bx = bX[c % 2]
            for k in (range(NDC) if DBG_PHASE != "dl" else []):
                g.op("V", (lambda k_: lambda e: e.scalar_tensor_tensor(
                    x[:, k_ * TC:(k_ + 1) * TC], xn[:, k_ * TC:(k_ + 1) * TC],
                    dvec[:, k_:k_ + 1], x[:, k_ * TC:(k_ + 1) * TC],
                    ALU.mult, ALU.add))(k),
                    reads=[bxn, bConst], writes=[bx[k]])
            def drfn(k):
                return lambda e: e.matmul(
                    ps[0][:, :], xpw[:, k * 160:k * 160 + 128],
                    xn[:, k * TC:(k + 1) * TC], start=(k == 0), stop=(k == NDC - 1))
            def bcfn(k):
                return lambda e: e.matmul(
                    ps[1][0:32, :], xpw[:, k * 160 + 128:k * 160 + 160],
                    xn[:, k * TC:(k + 1) * TC], start=(k == 0), stop=(k == NDC - 1))
            mms = []
            for k in range(NDC):
                mms.append(drfn(k))
                mms.append(bcfn(k))
            g.op("T", mms, reads=[bxn, bConst], writes=[bP[0], bP[1]])
            g.op("S", lambda e: e.copy(drs[:, :], ps[0][:, :]),
                 reads=[bP[0]], writes=[bDrs])
            g.op("S", lambda e: e.copy(bcs[:, :], ps[1][0:32, :]),
                 reads=[bP[1]], writes=[bBcs])
            for q in range(NQ):
                for j in range(GN):
                    g.push_unit(make_burst(c, q, j))
                for dc in range(NDC):
                    g.push_unit(make_delta(c, q, dc))
                    for j in range(GN):
                        g.push_unit(make_unit(c, q, dc, j))

        def make_burst(c, q, j):
            def burst():
                n = q * GN + j
                for row, dstbuf, dst in ((n, bBbh, Bbh), (16 + n, bCbh, Cbh)):
                    g.op("T", (lambda row_: lambda e: e.matmul(
                        ps[6][0:1, :], selc[0:32, row_:row_ + 1], bcs[:, :],
                        start=True, stop=True))(row),
                        reads=[bBcs, bConst], writes=[bP[6]])
                    g.op("S", lambda e: e.copy(stg[:, :], ps[6][0:1, :]),
                         reads=[bP[6]], writes=[bO1])
                    g.op("T", lambda e: e.matmul(ps[7][:, :], on1b[:, :], stg[:, :],
                                                 start=True, stop=True),
                         reads=[bO1, bConst], writes=[bP[7]])
                    g.op("S", (lambda dst_, j_: lambda e: e.copy(
                        dst_[:, j_ * TC:(j_ + 1) * TC], ps[7][:, :]))(dst, j),
                        reads=[bP[7]], writes=[dstbuf])
            return burst

        def make_delta(c, q, dc):
            def delta():
                bank = 6 if dc % 2 == 0 else 7
                g.op("T", (lambda bank_, dc_: lambda e: e.matmul(
                    ps[bank_][:, :], dtw[:, dc_ * 128:(dc_ + 1) * 128], drs[:, :],
                    start=True, stop=True))(bank, dc),
                    reads=[bDrs, bConst], writes=[bP[bank]])
                g.op("S", (lambda bank_, dc_: lambda e: e.activation(
                    out=sp_a[:, :], in_=ps[bank_][:, :], func=AF.Relu,
                    bias=dtb[:, dc_:dc_ + 1]))(bank, dc),
                    reads=[bP[bank], bConst], writes=[bSpA, bS0])
                g.op("S", (lambda bank_, dc_: lambda e: e.activation(
                    out=sp_b[:, :], in_=ps[bank_][:, :], func=AF.Abs,
                    bias=dtb[:, dc_:dc_ + 1]))(bank, dc),
                    reads=[bP[bank], bConst], writes=[bSpB, bS1])
                g.op("S", lambda e: e.activation(out=delta_t[:, :], in_=sp_b[:, :],
                                                 func=AF.Exp, scale=-1.0),
                     reads=[bSpB], writes=[bDelta])
                g.op("S", lambda e: e.activation(out=sp_b[:, :], in_=delta_t[:, :],
                                                 func=AF.Ln, bias=1.0),
                     reads=[bDelta, bConst], writes=[bSpB, bS1])
                g.op("V", lambda e: e.tensor_tensor(delta_t[:, :], sp_a[:, :],
                                                    sp_b[:, :], ALU.add),
                     reads=[bSpA, bSpB], writes=[bDelta])
                xn = xnb[c % 2]
                g.op("V", (lambda dc_: lambda e: e.tensor_tensor(
                    du_t[:, :], delta_t[:, :], xn[:, dc_ * TC:(dc_ + 1) * TC],
                    ALU.mult))(dc),
                    reads=[bDelta, bXN[c % 2]], writes=[bDu])
                if DBG_PHASE == "dl" and q == 0:
                    x = xb[c % 2]
                    g.op("V", (lambda dc_: lambda e: e.tensor_copy(
                        x[:, dc_ * TC:(dc_ + 1) * TC], delta_t[:, :]))(dc),
                        reads=[bDelta], writes=[bX[c % 2][dc]])
            return delta

        def make_unit(c, q, dc, j):
            def unit():
                U = unit_ctr[0]
                unit_ctr[0] += 1
                n = q * GN + j
                idx = dc * NST + n
                r = U % 2
                ra = U % RDA
                g.op("S", (lambda ra_, idx_: lambda e: e.activation(
                    out=dA_r[ra_][:, :], in_=delta_t[:, :], func=AF.Exp,
                    scale=negA[:, idx_:idx_ + 1]))(ra, idx),
                    reads=[bDelta, bPosA], writes=[bDA[ra]])
                g.op("V", (lambda r_, j_: lambda e: e.tensor_tensor(
                    dBu_r[r_][:, :], du_t[:, :], Bbh[:, j_ * TC:(j_ + 1) * TC],
                    ALU.mult))(r, j),
                    reads=[bDu, bBbh], writes=[bDBu[r]])
                carry = savepos.get(U - NDC * NST)
                if carry:
                    g.p.wait("V", "V", carry)
                else:
                    for s_, v_ in bHst.wr.d.items():
                        g.p.wait("V", s_, v_)
                g.op("V", (lambda r_, ra_, idx_: lambda e: e.tensor_tensor_scan(
                    h_r[r_][:, :], dA_r[ra_][:, :], dBu_r[r_][:, :],
                    h_state[:, idx_:idx_ + 1], ALU.mult, ALU.add))(r, ra, idx),
                    reads=[bDA[ra], bDBu[r]], writes=[bHr[r]])
                p = g.op("V", (lambda r_, idx_: lambda e: e.tensor_copy(
                    h_state[:, idx_:idx_ + 1], h_r[r_][:, TC - 1:TC]))(r, idx),
                    reads=[bHr[r]])
                savepos[U] = p
                if DBG_PHASE == "dl":
                    return
                x = xb[c % 2]
                dst = 0 if j == 0 else 1
                g.op("G", (lambda r_, j_, dst_: lambda e: e.tensor_tensor(
                    ymt[dst_][:, :], h_r[r_][:, :], Cbh[:, j_ * TC:(j_ + 1) * TC],
                    ALU.mult))(r, j, dst),
                    reads=[bHr[r], bCbh], writes=[bYmt[dst]])
                if j > 0:
                    g.op("G", lambda e: e.tensor_tensor(
                        ymt[0][:, :], ymt[0][:, :], ymt[1][:, :], ALU.add),
                        reads=[bYmt[1]], writes=[bYmt[0]])
                if j == GN - 1:
                    g.op("G", (lambda dc_: lambda e: e.tensor_tensor(
                        x[:, dc_ * TC:(dc_ + 1) * TC], x[:, dc_ * TC:(dc_ + 1) * TC],
                        ymt[0][:, :], ALU.add))(dc),
                        reads=[bYmt[0]], writes=[bX[c % 2][dc]])
            return unit

        load_x(0)
        load_x(1)
        wtiles = []
        for e_i in range(2):
            for mc in range(32):
                wtiles.append(wah[e_i][mc])
            for mq in range(16):
                wtiles.append(wbh[e_i][mq, 0])
                wtiles.append(wbh[e_i][mq, 1])
        wtiles_l = [wtiles]
        gws = WStream()
        if DBG_PHASE is not None:
            for c in range(NCH):
                layernorm(c)
                if DBG_PHASE == "ln":
                    xd, xnd = xb[c % 2], xnb[c % 2]
                    for k in range(NDC):
                        g.op("V", (lambda k_, xd_, xnd_: lambda e: e.tensor_copy(
                            xd_[:, k_ * TC:(k_ + 1) * TC],
                            xnd_[:, k_ * TC:(k_ + 1) * TC]))(k, xd, xnd),
                            reads=[bXN[c % 2]], writes=[bX[c % 2][k]])
                if DBG_PHASE == "ffa":
                    ws = WStream(); ws.tiles = wtiles
                    ffn_pass(c, 0, ws, 0, lambda: g.pop_tasks(2), only_a=True)
                    xd = xb[c % 2]
                    for k in range(NDC):
                        g.op("V", (lambda k_, xd_: lambda e: e.tensor_copy(
                            xd_[:, k_ * TC:(k_ + 1) * TC],
                            hbuf[:, k_ * TC:(k_ + 1) * TC]))(k, xd),
                            reads=[bH], writes=[bX[c % 2][k]])
                if DBG_PHASE == "ffb":
                    ws = WStream(); ws.tiles = wtiles
                    ffn_pass(c, 0, ws, 0, lambda: g.pop_tasks(2))
                    xd = xb[c % 2]
                    for k in range(NDC):
                        g.op("V", (lambda k_, xd_: lambda e: e.tensor_copy(
                            xd_[:, k_ * TC:(k_ + 1) * TC],
                            out0[:, k_ * TC:(k_ + 1) * TC]))(k, xd),
                            reads=[bO0], writes=[bX[c % 2][k]])
                if DBG_PHASE not in ("id", "ln", "ffa", "ffb"):
                    ws = WStream()
                    ffn_phase(c, ws)
                    g.flush()
                if DBG_PHASE in ("mamba", "ffn2", "dl"):
                    layernorm(c)
                    mamba_prologue(c)
                    g.flush()
                if DBG_PHASE == "ffn2":
                    layernorm(c)
                    ws = WStream()
                    ffn_phase(c, ws)
                    g.flush()
                store_x(c)
                if c + 2 < NCH:
                    load_x(c + 2)
            g.p.ops["Y"].append(("w", "O", g.p.ct["O"]))
        for it in (range(NCH + 1) if DBG_PHASE is None else []):
            if it < NCH:
                c = it
                layernorm(c)
                ffn_phase(c, gws)
                g.flush()
            if it >= 1:
                g.flush()
                layernorm(it - 1)
            if it < NCH:
                layernorm(c)
                gws.tiles.extend(wtiles)
                gws.pre = len(gws.tiles) - len(wtiles)
                gws.ensure(gws.pre + 3)
                mamba_prologue(c)
            if it >= 1:
                cm = it - 1
                ffn_phase(cm, gws)
                store_x(cm)
                if cm + 2 < NCH:
                    load_x(cm + 2)
        g.p.ops["Y"].append(("w", "O", g.p.ct["O"]))

        def replay(name):
            def run(eng):
                for item in g.p.ops[name]:
                    if item[0] == "w":
                        eng.wait_ge(sems[item[1]], item[2])
                    else:
                        inst = item[1](eng)
                        if item[2] is not None:
                            inst.then_inc(sems[item[2]], item[3])
            return run

        @block.tensor
        def _(tensor):
            replay("T")(tensor)

        @block.scalar
        def _(scalar):
            replay("S")(scalar)

        @block.vector
        def _(vector):
            replay("V")(vector)

        @block.gpsimd
        def _(gpsimd):
            replay("G")(gpsimd)

        @block.sync
        def _(sync):
            replay("Y")(sync)

    return nc


def _prep_weights(i):
    out = {}
    for e, (wa_k, ba_k, wb_k, bb_k) in enumerate(
            [("L0_ffn_W0a", "L0_ffn_b0a", "L0_ffn_W0b", "L0_ffn_b0b"),
             ("L0_ffn_W1a", "L0_ffn_b1a", "L0_ffn_W1b", "L0_ffn_b1b")]):
        wa = np.asarray(i[wa_k], dtype=np.float32)
        wb = np.asarray(i[wb_k], dtype=np.float32)
        bb = np.asarray(i[bb_k], dtype=np.float32)
        wah = np.zeros((32, 128, WCOL), dtype=BF)
        wah[:, :, :2048] = wa.reshape(32, 128, 16, 128).transpose(
            0, 3, 2, 1).reshape(32, 128, 2048).astype(BF)
        wbh = np.zeros((16, 2, 128, WCOL), dtype=BF)
        wbt = wb.reshape(16, 128, 32, 128).transpose(0, 2, 3, 1)
        wbh[:, 0, :, :2048] = wbt[:, 0:16].transpose(0, 2, 1, 3).reshape(
            16, 128, 2048).astype(BF)
        wbh[:, 1, :, :2048] = wbt[:, 16:32].transpose(0, 2, 1, 3).reshape(
            16, 128, 2048).astype(BF)
        wbh[:, 1, 0, 2048:2176] = bb.reshape(16, 128).astype(BF)
        out[f"wa{e}"] = wah
        out[f"wb{e}"] = wbh
        out[f"b{e}a_h"] = np.ascontiguousarray(
            np.asarray(i[ba_k], dtype=np.float32).reshape(32, 128).T)
    xpw = np.asarray(i["L0_attn_x_proj_w"], dtype=np.float32)
    out["xpw_h"] = np.ascontiguousarray(
        xpw.reshape(160, 16, 128).transpose(2, 1, 0)).reshape(128, 16 * 160).astype(BF)
    dtw = np.asarray(i["L0_attn_dt_proj_w"], dtype=np.float32)
    out["dtw_h"] = np.ascontiguousarray(
        dtw.reshape(16, 128, 128).transpose(2, 0, 1)).reshape(128, 2048).astype(BF)
    wgm = np.asarray(i["L0_ffn_Wg"], dtype=np.float32)
    out["wgf_h"] = np.ascontiguousarray(
        wgm.reshape(2, 16, 128).transpose(2, 1, 0).reshape(128, 32))
    selc = np.zeros((32, 32), dtype=BF)
    for r in range(32):
        selc[r, r] = 1.0
    out["selc_h"] = selc
    out["alog_h"] = np.ascontiguousarray(
        np.asarray(i["L0_attn_A_log"], dtype=np.float32).reshape(
            16, 128, 16).transpose(1, 0, 2)).reshape(128, 256)
    out["dvec_h"] = np.ascontiguousarray(
        np.asarray(i["L0_attn_D"], dtype=np.float32).reshape(16, 128).T)
    out["dtb_h"] = np.ascontiguousarray(
        np.asarray(i["L0_attn_dt_proj_b"], dtype=np.float32).reshape(16, 128).T)
    out["onesb_h"] = np.ones((1, TC), dtype=BF)
    out["on1f_h"] = np.ones((1, 128), dtype=np.float32)
    out["onesk_h"] = np.ones((128, 1), dtype=np.float32)
    return out


def kernel(**inputs):
    shared = _prep_weights(inputs)
    x = np.asarray(inputs["L0_attn_x"], dtype=np.float32)
    in_maps = []
    for b in range(8):
        m = dict(shared)
        m["xT"] = np.ascontiguousarray(x[b].T)
        in_maps.append(m)
    nc = build_program()
    res = run_bass_kernel_spmd(nc, in_maps, core_ids=list(range(8)))
    outs = [np.ascontiguousarray(np.asarray(r["yT"], dtype=np.float32).T)
            for r in res.results]
    return np.stack(outs, axis=0)
